# version 21
# baseline (speedup 1.0000x reference)
import os
import contextlib
import numpy as np
import ml_dtypes
import concourse.bass as bass
import concourse.mybir as mybir
from concourse.bass_utils import run_bass_kernel_spmd

F32 = mybir.dt.float32
BF16 = mybir.dt.bfloat16
AF = mybir.ActivationFunctionType
ALU = mybir.AluOpType
AX = mybir.AxisListType

NCH = 32
NT = 8
SLF = 256
KV_SLICES = [4, 5, 6, 7, 8, 9, 10, 11]
Q_SLICES = [0, 1, 2, 3]

EPS = 1e-5
SCALING = 2.0
INV32 = 1.0 / 32.0
BIG = 1e30


class Ctr:
    def __init__(self):
        self.v = 0

    def add(self, n=1):
        self.v += n
        return self.v


def build_nc(debug=False):
    nc = bass.Bass("TRN2", target_bir_lowering=False)

    xfm_d = nc.dram_tensor("xfm", [128, 32768], F32, kind="ExternalInput").ap()
    xtok_d = nc.dram_tensor("xtok", [1024, 4096], F32, kind="ExternalInput").ap()
    wq_d = nc.dram_tensor("wq", [12, 128, 8192], F32, kind="ExternalInput").ap()
    wo_d = nc.dram_tensor("wo", [128, 32768], F32, kind="ExternalInput").ap()
    bw_d = nc.dram_tensor("bw", [32, 128, 4096], F32, kind="ExternalInput").ap()
    a2_d = nc.dram_tensor("a2", [128, 4096], F32, kind="ExternalInput").ap()
    b2_d = nc.dram_tensor("b2", [128, 4096], F32, kind="ExternalInput").ap()
    rwt_d = nc.dram_tensor("rwt", [128, 256], F32, kind="ExternalInput").ap()
    bqkv_d = nc.dram_tensor("bqkv", [1, 3072], F32, kind="ExternalInput").ap()
    bo_d = nc.dram_tensor("bo", [1, 4096], F32, kind="ExternalInput").ap()
    ident_d = nc.dram_tensor("ident", [128, 128], F32, kind="ExternalInput").ap()
    out_d = nc.dram_tensor("out", [4096, 1024], F32, kind="ExternalOutput").ap()

    ikind = "ExternalOutput" if debug else "Internal"
    kcc_i = nc.dram_tensor("kcc_i", [1024, 1024], BF16, kind="Internal")
    kfmi = nc.dram_tensor("kfmi", [1024, 1024], BF16, kind="Internal")
    kcc_o = nc.dram_tensor("kcc_o", [4096, 1024], BF16, kind="Internal")
    vcc_i = nc.dram_tensor("vcc_i", [1024, 1024], BF16, kind="Internal")
    vcc_o = nc.dram_tensor("vcc_o", [4096, 1024], BF16, kind="Internal")
    kdump = nc.dram_tensor("kdump", [4096, 1024], BF16, kind=ikind) if debug else None
    vdump = nc.dram_tensor("vdump", [4096, 1024], BF16, kind=ikind) if debug else None
    qdump = nc.dram_tensor("qd", [128, 8192], BF16, kind=ikind) if debug else None
    aod = nc.dram_tensor("aod", [128, 8192], BF16, kind=ikind) if debug else None
    pd = nc.dram_tensor("pd", [1024, 4096], BF16, kind=ikind)
    sd = nc.dram_tensor("sd", [1024, 4096], BF16, kind=ikind)
    y2d = nc.dram_tensor("y2d", [4096, 1024], BF16, kind=ikind)
    wtd = nc.dram_tensor("wtd", [1024, 128], BF16, kind="Internal")

    BASE = (nc.sbuf_base + 31) & ~31
    TOP = 229344

    def sb(name, shape, dt, off):
        end = BASE + off + shape[1] * (2 if dt == BF16 else 4)
        assert end <= TOP, (name, end)
        return nc.alloc_sbuf_tensor_at(name, shape, dt, offset=BASE + off)

    consts = sb("consts", [128, 16], F32, 0)
    st = sb("st", [128, 32], F32, 64)
    dsum_tab = sb("dsum_tab", [128, 8], F32, 192)
    dinv_tab = sb("dinv_tab", [128, 8], F32, 224)
    dsumr = sb("dsumr", [1, 1024], BF16, 256)
    ones_r = sb("ones_r", [1, 1024], BF16, 320)
    ident = sb("ident_s", [128, 128], F32, 2368)
    ones16 = sb("ones16", [128, 16], F32, 2880)
    bqkv_r = sb("bqkv_r", [1, 3072], BF16, 2944)
    bo_r = sb("bo_r", [1, 4096], BF16, 9088)
    R1, R2, R3, RBW0 = 17280, 82816, 148352, 204608
    xfm = sb("xfm_s", [128, 32768], BF16, R1)
    kfm = sb("kfm", [128, 32768], BF16, R1)
    wo_s = sb("wo_s", [128, 32768], BF16, R1)
    sfm = sb("sfm", [128, 32768], BF16, R1)
    sqd2 = sb("sqd2", [128, 4096], BF16, R1)
    kvtok = sb("kvtok", [128, 16384], BF16, R2)
    kfmo = sb("kfmo", [128, 8192], BF16, R2 + 32768)
    vsb = sb("vsb", [128, 32768], BF16, R2)
    xtok_t = [sb(f"xtok{i}", [128, 4096], F32, R2 + i * 16384) for i in range(2)]
    y1_t = [sb(f"y1_{i}", [128, 4096], BF16, R2 + 32768 + i * 8192) for i in range(2)]
    sqd = sb("sqd", [128, 4096], BF16, R2 + 49152)
    bw_s = [sb("bw0", [128, 4096], BF16, RBW0),
            sb("bw1", [128, 4096], BF16, R2 + 0)]
    sqsl = [sb(f"sqsl{i}", [128, 1024], BF16, R2 + 8192 + i * 2048) for i in range(2)]
    srow = sb("srow", [1, 1024], F32, R2 + 12288)
    sqrow = sb("sqrow", [1, 1024], F32, R2 + 16384)
    bc_m = sb("bc_m", [128, 1024], F32, R2 + 32768)
    bc_rstd = sb("bc_rstd", [128, 1024], F32, R2 + 36864)
    bc_nmr = sb("bc_nmr", [128, 1024], F32, R2 + 40960)
    y2r = [sb(f"y2r{i}", [128, 1024], BF16, R2 + 45056 + i * 2048) for i in range(3)]
    oslot = [sb(f"oslot{i}", [128, 1024], F32, R2 + 51200 + i * 4096) for i in range(3)]
    ones_rf = sb("ones_rf", [1, 128], F32, R3 + 45312)
    ones_c = sb("ones_c", [128, 1], BF16, R3 + 45824)
    wq_s = [sb(f"wq{i}", [128, 8192], BF16, R3 + i * 16384) for i in range(2)]
    qfm = sb("qfm", [128, 8192], BF16, R3 + 32768)
    ptb = [sb(f"ptb{i}", [128, 512], BF16, R3 + i * 1024) for i in range(4)]
    ao = sb("ao", [128, 8192], BF16, R3 + 4096)
    P = sb("P", [128, 4096], BF16, R3 + 20480)
    s_t = [sb(f"s_{i}", [128, 4096], BF16, R3 + 32768 + i * 8192) for i in range(2)]
    b2_s = sb("b2_s", [128, 4096], BF16, R3 + 20480)
    rwt_s = sb("rwt_s", [128, 256], BF16, R3 + 28672)
    logit_s = sb("logit_s", [8, 1024], F32, R3 + 29184)
    ltok = sb("ltok", [128, 64], F32, R3 + 33280)
    tk = sb("tk", [128, 64], F32, R3 + 33536)
    tk2 = sb("tk2", [128, 320], F32, R3 + 33792)
    wet8 = sb("wet8", [128, 1024], BF16, R3 + 35072)
    wfm = sb("wfm", [128, 1024], BF16, R3 + 37120)
    midw = sb("midw", [128, 1024], BF16, R3 + 39168)
    y2_t = [sb(f"y2_{i}", [128, 1024], BF16, R3 + 41216 + i * 2048) for i in range(2)]
    a2_s = sb("a2_s", [128, 4096], BF16, R3 + 48064)

    C_EPS, C_NHALF, C_I32, C_ONE, C_NONE, C_SCL, C_NBIG, C_I4096, C_NI4096, C_ZERO, C_NB8 = range(11)
    CONST_VALS = [EPS, -0.5, INV32, 1.0, -1.0, SCALING, -BIG, 1.0 / 4096, -1.0 / 4096, 0.0, -8.0]

    def cc(i):
        return consts[:, i:i + 1]

    def sc_(j):
        return st[:, j:j + 1]

    def stc(qt, j):
        k = 6 + (qt % 2) * 6 + j
        return st[:, k:k + 1]

    ctx = contextlib.ExitStack()
    pm = ctx.enter_context(nc.psum_tensor("pm", [128, 4096], F32))

    def sem(name):
        return ctx.enter_context(nc.semaphore(name))

    sxf = [sem(f"sxf{i}") for i in range(4)]
    swq = [sem("swq0"), sem("swq1")]
    sbw = [sem("sbw0"), sem("sbw1")]
    swo = sem("swo")
    sab = sem("sab")
    gsm = sem("gsm")
    sdbg = sem("sdbg")
    skx, svx, skf, skfe = sem("skx"), sem("svx"), sem("skf"), sem("skfe")
    satk = [sem(f"satk{i}") for i in range(8)]
    svg = [sem(f"svg{i}") for i in range(8)]
    spt = [sem(f"spt{i}") for i in range(4)]
    spd = [sem("spd0"), sem("spd1")]
    sxt = [sem("sxt0"), sem("sxt1")]
    ssx = sem("ssx")
    ssf = [sem(f"ssf{i}") for i in range(4)]
    sy2 = sem("sy2")
    tst = sem("tst")
    sgc = sem("sgc")
    sy2r = [sem(f"sy2r{i}") for i in range(3)]
    sot = [sem(f"sot{i}") for i in range(3)]
    sog = [sem(f"sog{i}") for i in range(3)]
    spe = sem("spe")
    swt8 = sem("swt8")
    sa2 = sem("sa2")
    swt = sem("swt")
    ccs = sem("ccs")
    tmm = sem("tmm")
    sce = sem("sce")
    vec = sem("vec")
    scs = sem("scs")
    vss = sem("vss")

    c_tmm, c_sce, c_vec = Ctr(), Ctr(), Ctr()
    c_scs, c_vss = Ctr(), Ctr()

    A_pos = [list(range(NT)) if (si % 2 == 0) else list(range(NT - 1, -1, -1))
             for si in range(8)]

    def A_tmm(si, t):
        return si * NT + A_pos[si].index(t) + 1

    A_sce = A_tmm
    A_KV_END = 64

    def A_q(d):
        return A_KV_END + d + 1

    A_TMM_END = 72
    CCS_K, CCS_V = 1, 2

    if True:
        def phA_gpsimd(_e):
            _ = None
            g = _e
            for i, v in enumerate(CONST_VALS):
                g.memset(consts[:, i:i + 1], v)
            g.memset(ones_r[:, :], 1.0)
            g.memset(ones16[:, :], 1.0)
            g.dma_start(ident[:, :], ident_d[:, :]).then_inc(gsm, 16)
            g.dma_start(bqkv_r[:, :], bqkv_d[:, :]).then_inc(gsm, 16)
            g.dma_start(bo_r[:, :], bo_d[:, :]).then_inc(gsm, 16)
            for q4 in range(4):
                g.dma_start(xfm[:, q4 * 8192:(q4 + 1) * 8192],
                            xfm_d[:, q4 * 8192:(q4 + 1) * 8192]).then_inc(sxf[q4], 16)
            def wqload(i, s):
                if i >= 2:
                    j = i - 2
                    if j < 8:
                        g.wait_ge(tmm, (j + 1) * NT)
                    else:
                        g.wait_ge(tmm, A_KV_END + (j - 8) * 2 + 2)
                g.dma_start(wq_s[i % 2][:, :], wq_d[s, :, :]).then_inc(swq[i % 2], 16)

            allsl = KV_SLICES + Q_SLICES
            for i in range(6):
                wqload(i, allsl[i])
            g.wait_ge(skfe, 128)
            g.collective_compute(
                "AllGather", ALU.bypass,
                replica_groups=[[0, 1, 2, 3], [4, 5, 6, 7]],
                ins=[kfmi.ap()], outs=[kcc_o.ap()]).then_inc(ccs, 1)
            for i in range(6, 10):
                wqload(i, allsl[i])
            for i in range(10, 12):
                wqload(i, allsl[i])
            g.wait_ge(svx, 128)
            g.collective_compute(
                "AllGather", ALU.bypass,
                replica_groups=[[0, 1, 2, 3], [4, 5, 6, 7]],
                ins=[vcc_i.ap()], outs=[vcc_o.ap()]).then_inc(ccs, 1)
            g.dma_start(bw_s[0][:, :], bw_d[0, :, :]).then_inc(sbw[0], 16)

        def phA_tensor(_e):
            _ = None
            t = _e
            t.wait_ge(gsm, 48)
            for si, s in enumerate(KV_SLICES):
                t.wait_ge(swq[si % 2], 16 * (si // 2 + 1))
                F0 = s * SLF
                for ti in A_pos[si]:
                    if si == 0:
                        t.wait_ge(sxf[0], 16)
                    if si >= 1:
                        t.wait_ge(sce, A_sce(si - 1, ti))
                    ps = pm[:, ti * 512: ti * 512 + SLF]
                    for c in range(NCH):
                        if si == 0 and c % 8 == 0 and c > 0:
                            t.wait_ge(sxf[c // 8], 16)
                        t.matmul(ps,
                                 xfm[:, c * 1024 + ti * 128: c * 1024 + (ti + 1) * 128],
                                 wq_s[si % 2][:, c * SLF: (c + 1) * SLF],
                                 start=(c == 0), stop=False)
                    t.matmul(ps, ones_r[0:1, 0:128], bqkv_r[0:1, F0:F0 + SLF],
                             start=False, stop=True).then_inc(tmm, 1)
                    c_tmm.add()
            t.wait_ge(sce, A_KV_END)
            for qi, s in enumerate(Q_SLICES):
                si = 8 + qi
                t.wait_ge(swq[si % 2], 16 * (si // 2 + 1))
                for dsub in range(2):
                    d = qi * 2 + dsub
                    if d >= 4:
                        t.wait_ge(sce, A_q(d - 4))
                    for tq in range(2):
                        ps = pm[:, (d % 4) * 1024 + tq * 512: (d % 4) * 1024 + (tq + 1) * 512]
                        for c in range(NCH):
                            t.matmul(ps,
                                     wq_s[si % 2][:, c * SLF + dsub * 128:
                                                  c * SLF + dsub * 128 + 128],
                                     xfm[:, c * 1024 + tq * 512: c * 1024 + (tq + 1) * 512],
                                     start=(c == 0), stop=False)
                        ins = t.matmul(ps,
                                       bqkv_r[0:1, s * SLF + dsub * 128:
                                              s * SLF + dsub * 128 + 128],
                                       ones_r[0:1, tq * 512:(tq + 1) * 512],
                                       start=False, stop=True)
                        if tq == 1:
                            ins.then_inc(tmm, 1)
                            c_tmm.add()

        def phA_scalar(_e):
            _ = None
            s = _e
            for si, sl in enumerate(KV_SLICES):
                F0 = sl * SLF - 1024
                for ti in A_pos[si]:
                    s.wait_ge(tmm, A_tmm(si, ti))
                    s.activation(kvtok[:, ti * 2048 + F0: ti * 2048 + F0 + SLF],
                                 pm[:, ti * 512: ti * 512 + SLF],
                                 AF.Copy).then_inc(sce, 1)
                    c_sce.add()
            for d in range(8):
                s.wait_ge(tmm, A_q(d))
                s.activation(qfm[:, d * 1024:(d + 1) * 1024],
                             pm[:, (d % 4) * 1024:(d % 4 + 1) * 1024],
                             AF.Copy).then_inc(sce, 1)
                c_sce.add()
            s.wait_ge(ccs, CCS_K)
            s.wait_ge(tmm, A_TMM_END)
            for d in range(8):
                for rk in range(1, 4, 2):
                    s.dma_start(kfm[:, d * 4096 + rk * 1024: d * 4096 + (rk + 1) * 1024],
                                kcc_o.ap()[rk * 1024 + d * 128: rk * 1024 + (d + 1) * 128, :]
                                ).then_inc(satk[d], 16)

        def phA_sync(_e):
            _ = None
            s = _e
            for ti in range(NT):
                s.wait_ge(sce, A_sce(3, ti))
                s.dma_start(kcc_i.ap()[ti * 128:(ti + 1) * 128, :],
                            kvtok[:, ti * 2048: ti * 2048 + 1024]).then_inc(skx, 16)
            s.wait_ge(skx, 128)
            for d in range(8):
                s.dma_start(kfmo[:, d * 1024:(d + 1) * 1024],
                            kcc_i.ap()[0:1024, d * 128:(d + 1) * 128],
                            transpose=True).then_inc(skf, 16)
            s.wait_ge(skf, 128)
            for d in range(8):
                s.dma_start(kfmi.ap()[d * 128:(d + 1) * 128, :],
                            kfmo[:, d * 1024:(d + 1) * 1024]).then_inc(skfe, 16)
            for ti in range(NT):
                s.wait_ge(sce, A_sce(7, ti))
                s.dma_start(vcc_i.ap()[ti * 128:(ti + 1) * 128, :],
                            kvtok[:, ti * 2048 + 1024: ti * 2048 + 2048]).then_inc(svx, 16)
            s.wait_ge(ccs, CCS_K)
            s.wait_ge(tmm, A_TMM_END)
            for d in range(8):
                for rk in range(0, 4, 2):
                    s.dma_start(kfm[:, d * 4096 + rk * 1024: d * 4096 + (rk + 1) * 1024],
                                kcc_o.ap()[rk * 1024 + d * 128: rk * 1024 + (d + 1) * 128, :]
                                ).then_inc(satk[d], 16)

    B_tmm0 = 72

    def B_tmm_S(qt, kh):
        return B_tmm0 + 2 * qt + kh + 1

    def B_tmm_PV(half, kc):
        return B_tmm0 + 16 + half * 32 + kc + 1

    B_sce0 = 72

    def B_sce_exp(qt, kh):
        return B_sce0 + 2 * qt + kh + 1

    B_vec0 = 0

    def B_vec_dinv(qt):
        return B_vec0 + qt + 1

    def B_vec_pv(half, d):
        return B_vec0 + 8 + half * 8 + d + 1

    def PT_TGT(half, kc):
        return 16 * (half * 8 + kc // 4 + 1)

    if True:
        def phB_sync(_e):
            _ = None
            s = _e
            for qt in range(4):
                for kh in range(2):
                    s.wait_ge(sce, B_sce_exp(qt, kh))
                    s.dma_start(pd.ap()[qt * 128:(qt + 1) * 128,
                                        kh * 2048:(kh + 1) * 2048],
                                P[:, kh * 2048:(kh + 1) * 2048]).then_inc(spd[kh], 16)
            s.wait_ge(spd[0], 64)
            for kc in range(4):
                s.dma_start(ptb[kc % 4][:, :],
                            pd.ap()[0:512, kc * 128:(kc + 1) * 128],
                            transpose=True).then_inc(spt[kc % 4], 16)
            s.wait_ge(ccs, CCS_V)
            s.wait_ge(svx, 128)
            s.wait_ge(skfe, 128)
            for g in range(0, 8, 2):
                for kc in range(4 * g, 4 * g + 4):
                    s.dma_start(vsb[:, kc * 1024:(kc + 1) * 1024],
                                vcc_o.ap()[kc * 128:(kc + 1) * 128, :]).then_inc(svg[g], 16)
            for qt in range(4, NT):
                for kh in range(2):
                    s.wait_ge(sce, B_sce_exp(qt, kh))
                    s.dma_start(pd.ap()[qt * 128:(qt + 1) * 128,
                                        kh * 2048:(kh + 1) * 2048],
                                P[:, kh * 2048:(kh + 1) * 2048]).then_inc(spd[kh], 16)
            for half in range(2):
                for kc in range(4 if half == 0 else 0, 32):
                    idx = half * 32 + kc
                    if idx >= 4:
                        pi = idx - 4
                        s.wait_ge(tmm, B_tmm_PV(pi // 32, pi % 32))
                    s.wait_ge(spd[kc // 16], 64 if half == 0 else 128)
                    s.dma_start(ptb[kc % 4][:, :],
                                pd.ap()[half * 512:(half + 1) * 512,
                                        kc * 128:(kc + 1) * 128],
                                transpose=True).then_inc(spt[kc % 4], 16)

        def phB_scalar(_e):
            _ = None
            s = _e
            for qt in range(NT):
                s.wait_ge(tmm, B_tmm_S(qt, 0))
                if qt >= 1:
                    s.wait_ge(spd[0], 16 * qt)
                    s.wait_ge(vec, B_vec_dinv(qt - 1))
                s.activation(P[:, 0:2048], pm[:, 0:2048], AF.Exp,
                             bias=cc(C_NB8), scale=cc(C_I32),
                             accum_out=sc_(2)).then_inc(sce, 1)
                c_sce.add()
                s.wait_ge(tmm, B_tmm_S(qt, 1))
                if qt >= 1:
                    s.wait_ge(spd[1], 16 * qt)
                s.activation(P[:, 2048:4096], pm[:, 2048:4096], AF.Exp,
                             bias=cc(C_NB8), scale=cc(C_I32),
                             accum_out=sc_(5)).then_inc(sce, 1)
                c_sce.add()
            s.wait_ge(ccs, CCS_V)
            s.wait_ge(skx, 128)
            s.wait_ge(svx, 128)
            s.wait_ge(skfe, 128)
            for g in range(1, 8, 2):
                for kc in range(4 * g, 4 * g + 4):
                    s.dma_start(vsb[:, kc * 1024:(kc + 1) * 1024],
                                vcc_o.ap()[kc * 128:(kc + 1) * 128, :]).then_inc(svg[g], 16)

        def phB_gpsimd(_e):
            _ = None
            g = _e
            g.wait_ge(tmm, B_tmm_S(7, 1))
            _wo_gate = [None, (0, 8), (0, 24), (1, 8)]
            for i in range(4):
                if i >= 1:
                    g.wait_ge(tmm, B_tmm_PV(*_wo_gate[i]))
                g.dma_start(wo_s[:, i * 8192:(i + 1) * 8192],
                            wo_d[:, i * 8192:(i + 1) * 8192]).then_inc(swo, 16)
            g.wait_ge(tmm, B_tmm_PV(1, 31))
            g.dma_start(b2_s[:, :], b2_d[:, :]).then_inc(sab, 16)
            g.dma_start(rwt_s[:, :], rwt_d[:, :]).then_inc(sab, 16)
            if debug:
                g.dma_start(kdump.ap()[:, :], kcc_o.ap()[:, :]).then_inc(sdbg, 16)
                g.dma_start(vdump.ap()[:, :], vcc_o.ap()[:, :]).then_inc(sdbg, 16)
                g.dma_start(qdump.ap()[:, :], qfm[:, :]).then_inc(sdbg, 16)
                g.wait_ge(vec, B_vec_pv(1, 7))
                g.dma_start(aod.ap()[:, :], ao[:, :]).then_inc(sdbg, 16)
                g.wait_ge(sdbg, 64)

        def phB_tensor(_e):
            _ = None
            t = _e
            t.wait_ge(sce, 72)
            for qt in range(NT):
                for kh in range(2):
                    if qt >= 1:
                        t.wait_ge(sce, B_sce_exp(qt - 1, kh))
                    for d in range(8):
                        if qt == 0 and kh == 0:
                            t.wait_ge(satk[d], 64)
                        lhs = qfm[:, d * 1024 + qt * 128: d * 1024 + (qt + 1) * 128]
                        for kb in range(4 * kh, 4 * kh + 4):
                            ins = t.matmul(
                                pm[:, kb * 512:(kb + 1) * 512], lhs,
                                kfm[:, d * 4096 + kb * 512: d * 4096 + (kb + 1) * 512],
                                start=(d == 0), stop=(d == 7))
                            if d == 7 and kb == 4 * kh + 3:
                                ins.then_inc(tmm, 1)
                                c_tmm.add()
            t.wait_ge(sce, B_sce_exp(7, 1))
            for half in range(2):
                if half == 1:
                    for d in range(8):
                        t.wait_ge(vec, B_vec_pv(0, d))
                for kc in range(32):
                    if half == 0 and kc % 4 == 0:
                        t.wait_ge(svg[kc // 4], 64)
                    t.wait_ge(spt[kc % 4], PT_TGT(half, kc))
                    for d in range(8):
                        ins = t.matmul(
                            pm[:, d * 512:(d + 1) * 512],
                            vsb[:, kc * 1024 + d * 128: kc * 1024 + (d + 1) * 128],
                            ptb[kc % 4][:, :],
                            start=(kc == 0), stop=(kc == 31))
                        if d == 7:
                            ins.then_inc(tmm, 1)
                            c_tmm.add()

        def phB_vector(_e):
            _ = None
            v = _e
            for qt in range(NT):
                v.wait_ge(sce, B_sce_exp(qt, 1))
                v.tensor_add(dsum_tab[:, qt:qt + 1], sc_(2), sc_(5)).then_inc(vss, 1)
                v.wait_ge(vss, c_vss.add())
                v.reciprocal(dinv_tab[:, qt:qt + 1],
                             dsum_tab[:, qt:qt + 1]).then_inc(vec, 1)
                c_vec.add()
            for half in range(2):
                for d in range(8):
                    v.wait_ge(tmm, B_tmm_PV(half, 31))
                    v.tensor_copy(ao[:, d * 1024 + half * 512: d * 1024 + half * 512 + 512],
                                  pm[:, d * 512:(d + 1) * 512]).then_inc(vec, 1)
                    c_vec.add()

    C_tmm0 = 152

    def C_tmm(qt, half):
        return C_tmm0 + 1 + 2 * qt + half + 1

    C_vec0 = 24

    def C_vec_evac(qt, half):
        return C_vec0 + 1 + 5 * qt + half + 1

    def C_vec_var(qt):
        return C_vec0 + 1 + 5 * qt + 4

    def C_vec_nmr(qt):
        return C_vec0 + 1 + 5 * qt + 5

    C_sce0 = 88

    def C_sce_sq(qt):
        return C_sce0 + 2 * qt + 1

    def C_sce_apply(qt):
        return C_sce0 + 2 * qt + 2

    C_scs0 = 0

    def C_scs_rstd(qt):
        return C_scs0 + 2 * qt + 2

    if True:
        def phC_sync(_e):
            _ = None
            s = _e
            s.wait_ge(tmm, 152)
            for qt in range(2):
                s.dma_start(xtok_t[qt % 2][:, :],
                            xtok_d[qt * 128:(qt + 1) * 128, :]).then_inc(sxt[qt % 2], 16)
            for qt in range(NT):
                if qt + 2 < NT:
                    s.wait_ge(vec, C_vec_evac(qt, 1))
                    s.dma_start(xtok_t[qt % 2][:, :],
                                xtok_d[(qt + 2) * 128:(qt + 3) * 128, :]).then_inc(sxt[qt % 2], 16)
                s.wait_ge(sce, C_sce_apply(qt))
                if qt >= 1:
                    s.wait_ge(ssx, 16 * qt)
                s.dma_start(sd.ap()[qt * 128:(qt + 1) * 128, :],
                            s_t[qt % 2][:, :]).then_inc(ssx, 16)

        def phC_tensor(_e):
            _ = None
            t = _e
            t.wait_ge(swo, 64)
            t.wait_ge(vec, B_vec0 + 24)
            for j in range(NT):
                ins = t.transpose(pm[0:1, j * 128:(j + 1) * 128],
                                  dsum_tab[:, j:j + 1], ident[:, :])
                if j == NT - 1:
                    ins.then_inc(tmm, 1)
                    c_tmm.add()
            t.wait_ge(vec, C_vec0 + 1)
            for qt in range(NT):
                for half in range(2):
                    if qt >= 1:
                        t.wait_ge(vec, C_vec_evac(qt - 1, half))
                    for d in range(8):
                        lhs = ao[:, d * 1024 + qt * 128: d * 1024 + (qt + 1) * 128]
                        for cb in range(4):
                            col = half * 4 + cb
                            t.matmul(pm[:, col * 512:(col + 1) * 512], lhs,
                                     wo_s[:, d * 4096 + half * 2048 + cb * 512:
                                          d * 4096 + half * 2048 + (cb + 1) * 512],
                                     start=(d == 0), stop=False)
                    for cb in range(4):
                        col = half * 4 + cb
                        ins = t.matmul(pm[:, col * 512:(col + 1) * 512],
                                       dsumr[0:1, qt * 128:(qt + 1) * 128],
                                       bo_r[0:1, half * 2048 + cb * 512:
                                            half * 2048 + (cb + 1) * 512],
                                       start=False, stop=True)
                        if cb == 3:
                            ins.then_inc(tmm, 1)
                            c_tmm.add()

        def phC_vector(_e):
            _ = None
            v = _e
            v.wait_ge(tmm, C_tmm0 + 1)
            v.tensor_copy(dsumr[0:1, :], pm[0:1, 0:1024]).then_inc(vec, 1)
            c_vec.add()
            for qt in range(NT):
                for half in range(2):
                    v.wait_ge(tmm, C_tmm(qt, half))
                    v.wait_ge(sxt[qt % 2], 16 * (qt // 2 + 1))
                    if half == 0 and qt >= 2:
                        v.wait_ge(sce, C_sce_apply(qt - 2))
                    v.scalar_tensor_tensor(
                        y1_t[qt % 2][:, half * 2048:(half + 1) * 2048],
                        pm[:, half * 2048:(half + 1) * 2048],
                        dinv_tab[:, qt:qt + 1],
                        xtok_t[qt % 2][:, half * 2048:(half + 1) * 2048],
                        ALU.mult, ALU.add,
                        accum_out=stc(qt, half)).then_inc(vec, 1)
                    c_vec.add()
                v.wait_ge(vec, C_vec_evac(qt, 1))
                v.tensor_add(stc(qt, 2), stc(qt, 0), stc(qt, 1)).then_inc(vss, 1)
                v.wait_ge(vss, c_vss.add())
                v.tensor_scalar_mul(stc(qt, 2), stc(qt, 2), cc(C_NI4096)).then_inc(vec, 1)
                c_vec.add()
                v.wait_ge(vec, c_vec.v)
                v.wait_ge(sce, C_sce_sq(qt))
                v.tensor_scalar_mul(stc(qt, 4), stc(qt, 3), cc(C_I4096)).then_inc(vss, 1)
                v.wait_ge(vss, c_vss.add())
                v.tensor_mul(stc(qt, 5), stc(qt, 2), stc(qt, 2)).then_inc(vss, 1)
                v.wait_ge(vss, c_vss.add())
                v.tensor_sub(stc(qt, 4), stc(qt, 4), stc(qt, 5)).then_inc(vec, 1)
                c_vec.add()
                v.wait_ge(vec, c_vec.v)
                v.wait_ge(scs, C_scs_rstd(qt))
                v.tensor_mul(stc(qt, 5), stc(qt, 2), stc(qt, 4)).then_inc(vec, 1)
                c_vec.add()

        def phC_scalar(_e):
            _ = None
            s = _e
            if debug:
                s.wait_ge(sdbg, 64)
            for qt in range(NT):
                s.wait_ge(vec, C_vec_evac(qt, 1))
                s.activation(sqd[:, :], y1_t[qt % 2][:, :], AF.Square,
                             accum_out=stc(qt, 3)).then_inc(sce, 1)
                c_sce.add()
                s.wait_ge(vec, C_vec_var(qt))
                s.activation(stc(qt, 4), stc(qt, 4), AF.Ln, bias=cc(C_EPS)).then_inc(scs, 1)
                s.wait_ge(scs, c_scs.add())
                s.activation(stc(qt, 4), stc(qt, 4), AF.Exp,
                             scale=cc(C_NHALF)).then_inc(scs, 1)
                c_scs.add()
                s.wait_ge(vec, C_vec_nmr(qt))
                s.wait_ge(scs, C_scs_rstd(qt))
                if qt >= 2:
                    s.wait_ge(ssx, 16 * (qt - 1))
                s.activation(s_t[qt % 2][:, :], y1_t[qt % 2][:, :], AF.Identity,
                             bias=stc(qt, 5), scale=stc(qt, 4)).then_inc(sce, 1)
                c_sce.add()

    D_tmm0 = 169
    D_TMM_ROUTER = D_tmm0 + 1
    D_TMM_MID = D_tmm0 + 2

    def D_TMM_LT(i):
        return D_tmm0 + 2 + i + 1

    def D_TMM_FFN(fch):
        return D_tmm0 + 10 + fch + 1

    D_vec0 = 65
    D_VEC_REVAC = D_vec0 + 1

    def D_VEC_LTOK(i):
        return D_vec0 + 1 + i + 1

    D_VEC_MIDW = D_vec0 + 10

    def D_VEC_FFN(fch):
        return D_vec0 + 10 + fch + 1

    D_sce0 = 104
    D_SCE_TKEXP = D_sce0 + 1

    def D_SCE_SQ(j):
        return 106 + j

    TMM_BCS = 212
    TMM_BCQ = 213
    E_VEC_SROW = 108
    E_VEC_SQROW = 109
    E_VEC_VAR = 113
    E_VEC_NMR = 114
    E_SCE_EXP = 138

    def E_VEC_A1(j):
        assert j % 2 == 0
        return 115 + j

    def E_VEC_A2(j):
        assert j % 2 == 0
        return 116 + j

    def E_SPE_A1(j):
        assert j % 2 == 1
        return j

    def E_SPE_A2(j):
        assert j % 2 == 1
        return j + 1

    def _st_sem(j):
        return sot[j % 3] if j % 2 == 0 else sog[j % 3]

    def _st_cnt(j):
        return 16 * len([m for m in range(j % 3, j + 1, 3) if m % 2 == j % 2])

    def SFM_TGT(c):
        return 16 * (c // 4 + 1)

    tkm = {}

    if True:
        def phD_sync(_e):
            _ = None
            s = _e
            s.wait_ge(ssx, 128)
            for c in range(NCH):
                if c >= 4:
                    s.wait_ge(ssf[c % 4], SFM_TGT(c) - 16)
                s.dma_start(sfm[:, c * 1024:(c + 1) * 1024],
                            sd.ap()[0:1024, c * 128:(c + 1) * 128],
                            transpose=True).then_inc(ssf[c % 4], 16)
            s.wait_ge(swt8, 128)
            s.dma_start(wfm[:, :], wtd.ap()[0:1024, 0:128],
                        transpose=True).then_inc(swt, 16)
            for fch in range(32):
                s.wait_ge(vec, D_VEC_FFN(fch))
                if fch >= 1:
                    s.wait_ge(sy2, 16 * fch)
                s.dma_start(y2d.ap()[fch * 128:(fch + 1) * 128, :],
                            y2_t[fch % 2][:, :]).then_inc(sy2, 16)

        def phD_tensor(_e):
            _ = None
            t = _e
            t.wait_ge(sab, 32)
            for tq in range(2):
                for c in range(NCH):
                    t.wait_ge(ssf[c % 4], SFM_TGT(c))
                    ins = t.matmul(pm[0:8, 3072 + tq * 512: 3072 + (tq + 1) * 512],
                                   rwt_s[:, c * 8:(c + 1) * 8],
                                   sfm[:, c * 1024 + tq * 512: c * 1024 + tq * 512 + 512],
                                   start=(c == 0), stop=(c == NCH - 1))
                    if tq == 1 and c == NCH - 1:
                        ins.then_inc(tmm, 1)
                        c_tmm.add()
            t.wait_ge(sa2, 16)
            for tq in range(2):
                for c in range(NCH):
                    ins = t.matmul(pm[:, 2048 + tq * 512: 2048 + (tq + 1) * 512],
                                   a2_s[:, c * 128:(c + 1) * 128],
                                   sfm[:, c * 1024 + tq * 512: c * 1024 + tq * 512 + 512],
                                   start=(c == 0), stop=(c == NCH - 1))
                    if tq == 1 and c == NCH - 1:
                        ins.then_inc(tmm, 1)
                        c_tmm.add()
            t.wait_ge(vec, D_VEC_REVAC)
            for i in range(8):
                if i >= 1:
                    t.wait_ge(vec, D_VEC_LTOK(i - 1))
                t.transpose(pm[:, 3072 + i * 8: 3072 + (i + 1) * 8],
                            logit_s[0:8, i * 128:(i + 1) * 128],
                            ident[0:8, 0:8]).then_inc(tmm, 1)
                c_tmm.add()
            def _stats(j, last):
                t.wait_ge(sce, D_SCE_SQ(j))
                if j == 0:
                    t.wait_ge(sgc, 2)
                for tq in range(2):
                    t.matmul(pm[0:1, 2048 + tq * 512: 2048 + (tq + 1) * 512],
                             ones_c[:, :],
                             y2_t[j % 2][:, tq * 512:(tq + 1) * 512],
                             start=(j == 0), stop=(j == 31))
                for tq in range(2):
                    ins = t.matmul(pm[0:1, 3072 + tq * 512: 3072 + (tq + 1) * 512],
                                   ones_c[:, :],
                                   sqsl[j % 2][:, tq * 512:(tq + 1) * 512],
                                   start=(j == 0), stop=(j == 31))
                    if last and tq == 1:
                        ins.then_inc(tst, 1)

            for fch in range(32):
                t.wait_ge(sbw[fch % 2], 16 * (fch // 2 + 1))
                if fch >= 2:
                    t.wait_ge(vec, D_VEC_FFN(fch - 2))
                for tq in range(2):
                    for c in range(NCH):
                        t.matmul(pm[:, (fch % 2) * 1024 + tq * 512:
                                     (fch % 2) * 1024 + (tq + 1) * 512],
                                 bw_s[fch % 2][:, c * 128:(c + 1) * 128],
                                 sfm[:, c * 1024 + tq * 512: c * 1024 + tq * 512 + 512],
                                 start=(c == 0), stop=False)
                if fch >= 2:
                    _stats(fch - 2, False)
                if fch == 0:
                    t.wait_ge(swt, 16)
                    t.wait_ge(vec, D_VEC_MIDW)
                for tq in range(2):
                    ins = t.matmul(pm[:, (fch % 2) * 1024 + tq * 512:
                                       (fch % 2) * 1024 + (tq + 1) * 512],
                                   b2_s[:, fch * 128:(fch + 1) * 128],
                                   midw[:, tq * 512:(tq + 1) * 512],
                                   start=False, stop=True)
                    if tq == 1:
                        ins.then_inc(tmm, 1)
                        c_tmm.add()
            _stats(30, False)
            _stats(31, True)
            t.wait_ge(vec, E_VEC_SQROW)
            t.wait_ge(vec, D_VEC_FFN(30))
            for h in range(2):
                ins = t.matmul(pm[:, h * 512:(h + 1) * 512], ones_rf[:, :],
                               srow[0:1, h * 512:(h + 1) * 512],
                               start=True, stop=True)
                if h == 1:
                    ins.then_inc(tmm, 1)
                    c_tmm.add()
            t.wait_ge(vec, D_VEC_FFN(31))
            for h in range(2):
                ins = t.matmul(pm[:, 1024 + h * 512: 1024 + (h + 1) * 512],
                               ones_rf[:, :],
                               sqrow[0:1, h * 512:(h + 1) * 512],
                               start=True, stop=True)
                if h == 1:
                    ins.then_inc(tmm, 1)
                    c_tmm.add()

        def phD_vector(_e):
            _ = None
            v = _e
            SH = tk2[:, 0:64]
            EQ = tk2[:, 64:128]
            W = tk2[:, 128:192]
            E = tk2[:, 192:256]
            v.wait_ge(tmm, D_TMM_ROUTER)
            v.tensor_copy(logit_s[0:8, :], pm[0:8, 3072:4096]).then_inc(vec, 1)
            c_vec.add()
            for i in range(8):
                v.wait_ge(tmm, D_TMM_LT(i))
                v.tensor_copy(ltok[:, i * 8:(i + 1) * 8],
                              pm[:, 3072 + i * 8: 3072 + (i + 1) * 8]).then_inc(vec, 1)
                c_vec.add()
            v.wait_ge(vec, c_vec.v)
            for i in range(8):
                v.reduce_max(tk[:, i:i + 1], ltok[:, i * 8:(i + 1) * 8],
                             axis=AX.X, negate=True).then_inc(vss, 1)
                c_vss.add()
            v.wait_ge(vss, c_vss.v)
            for i in range(8):
                v.tensor_scalar_add(SH[:, i * 8:(i + 1) * 8],
                                    ltok[:, i * 8:(i + 1) * 8],
                                    tk[:, i:i + 1]).then_inc(vss, 1)
                c_vss.add()
            v.wait_ge(vss, c_vss.v)
            v.tensor_scalar(EQ, SH, cc(C_ZERO), None, ALU.is_equal).then_inc(vss, 1)
            v.wait_ge(vss, c_vss.add())
            v.scalar_tensor_tensor(W, EQ, cc(C_NBIG), SH, ALU.mult, ALU.add).then_inc(vss, 1)
            v.wait_ge(vss, c_vss.add())
            for i in range(8):
                v.reduce_max(tk[:, 8 + i:9 + i], W[:, i * 8:(i + 1) * 8],
                             axis=AX.X).then_inc(vss, 1)
                c_vss.add()
            v.wait_ge(vss, c_vss.v)
            for i in range(8):
                v.tensor_scalar(EQ[:, i * 8:(i + 1) * 8], SH[:, i * 8:(i + 1) * 8],
                                tk[:, 8 + i:9 + i], None, ALU.is_ge).then_inc(vss, 1)
                c_vss.add()
            tkm["exp_in"] = c_vss.v
            v.wait_ge(sce, D_SCE_TKEXP)
            v.tensor_mul(W, E, EQ).then_inc(vss, 1)
            v.wait_ge(vss, c_vss.add())
            for i in range(8):
                v.reduce_sum(tk[:, 16 + i:17 + i], W[:, i * 8:(i + 1) * 8],
                             axis=AX.X).then_inc(vss, 1)
                c_vss.add()
            v.wait_ge(vss, c_vss.v)
            v.reciprocal(tk[:, 24:32], tk[:, 16:24]).then_inc(vss, 1)
            v.wait_ge(vss, c_vss.add())
            v.tensor_scalar_mul(tk[:, 24:32], tk[:, 24:32], cc(C_SCL)).then_inc(vss, 1)
            v.wait_ge(vss, c_vss.add())
            for i in range(8):
                v.tensor_scalar_mul(W[:, i * 8:(i + 1) * 8], W[:, i * 8:(i + 1) * 8],
                                    tk[:, 24 + i:25 + i]).then_inc(vss, 1)
                c_vss.add()
            v.wait_ge(vss, c_vss.v)
            for i in range(8):
                for e in range(8):
                    v.tensor_scalar_mul(wet8[:, i * 128 + e * 16: i * 128 + (e + 1) * 16],
                                        ones16[:, :],
                                        W[:, i * 8 + e: i * 8 + e + 1]).then_inc(vss, 1)
                    c_vss.add()
            tkm["wet8"] = c_vss.v
            v.wait_ge(swt, 16)
            v.wait_ge(tmm, D_TMM_MID)
            v.tensor_mul(midw[:, :], pm[:, 2048:3072], wfm[:, :]).then_inc(vec, 1)
            c_vec.add()
            assert c_vec.v == D_VEC_MIDW
            for fch in range(32):
                v.wait_ge(tmm, D_TMM_FFN(fch))
                if fch >= 2:
                    v.wait_ge(sy2, 16 * (fch - 1))
                v.scalar_tensor_tensor(
                    y2_t[fch % 2][:, :],
                    pm[:, (fch % 2) * 1024:(fch % 2 + 1) * 1024],
                    cc(C_ONE),
                    sfm[:, fch * 1024:(fch + 1) * 1024],
                    ALU.mult, ALU.add).then_inc(vec, 1)
                c_vec.add()

        def phD_gpsimd(_e):
            _ = None
            g = _e
            g.wait_ge(sce, C_sce_apply(7))
            g.wait_ge(ssx, 128)
            g.memset(ones_c[:, :], 1.0).then_inc(sgc, 1)
            g.memset(ones_rf[:, :], 1.0).then_inc(sgc, 1)
            g.dma_start(a2_s[:, :], a2_d[:, :]).then_inc(sa2, 16)
            g.dma_start(bw_s[1][:, :], bw_d[1, :, :]).then_inc(sbw[1], 16)
            g.wait_ge(vss, tkm["wet8"])
            for i in range(8):
                g.dma_start(wtd.ap()[i * 128:(i + 1) * 128, :],
                            wet8[:, i * 128:(i + 1) * 128]).then_inc(swt8, 16)
            for fch in range(2, 32):
                g.wait_ge(vec, D_VEC_FFN(fch - 2))
                g.dma_start(bw_s[fch % 2][:, :], bw_d[fch, :, :]).then_inc(sbw[fch % 2], 16)

        def phD_scalar(_e):
            _ = None
            s = _e
            s.wait_ge(vss, tkm["exp_in"])
            s.activation(tk2[:, 192:256], tk2[:, 0:64], AF.Exp).then_inc(sce, 1)
            c_sce.add()
            for j in range(32):
                s.wait_ge(vec, D_VEC_FFN(j))
                s.activation(sqsl[j % 2][:, :], y2_t[j % 2][:, :],
                             AF.Square).then_inc(sce, 1)
                c_sce.add()

    if True:
        def phE_sync(_e):
            _ = None
            s = _e

            def _ostore(k):
                s.wait_ge(spe, E_SPE_A2(k))
                s.dma_start(out_d[k * 128:(k + 1) * 128, :],
                            oslot[k % 3][:, :]).then_inc(sog[k % 3], 16)

            for j in range(1, 32, 2):
                s.wait_ge(sy2, 16 * (j + 1))
                if j >= 3:
                    s.wait_ge(vec, E_VEC_A1(j - 3))
                s.dma_start(y2r[j % 3][:, :],
                            y2d.ap()[j * 128:(j + 1) * 128, :]).then_inc(sy2r[j % 3], 16)
                if j >= 5:
                    _ostore(j - 4)
            _ostore(29)
            _ostore(31)
            for ss in range(3):
                s.wait_ge(sot[ss], _st_cnt(max(m for m in range(32) if m % 3 == ss and m % 2 == 0)))
                s.wait_ge(sog[ss], _st_cnt(max(m for m in range(32) if m % 3 == ss and m % 2 == 1)))

        def phE_vector(_e):
            _ = None
            v = _e
            v.wait_ge(tst, 1)
            v.tensor_copy(srow[0:1, :], pm[0:1, 2048:3072]).then_inc(vec, 1)
            c_vec.add()
            v.tensor_copy(sqrow[0:1, :], pm[0:1, 3072:4096]).then_inc(vec, 1)
            c_vec.add()
            assert c_vec.v == E_VEC_SQROW
            v.wait_ge(tmm, TMM_BCS)
            v.tensor_scalar_mul(bc_m[:, :], pm[:, 0:1024], cc(C_I4096)).then_inc(vec, 1)
            c_vec.add()
            v.wait_ge(tmm, TMM_BCQ)
            v.tensor_scalar_mul(bc_rstd[:, :], pm[:, 1024:2048],
                                cc(C_I4096)).then_inc(vec, 1)
            v.wait_ge(vec, c_vec.add())
            v.tensor_mul(bc_nmr[:, :], bc_m[:, :], bc_m[:, :]).then_inc(vec, 1)
            v.wait_ge(vec, c_vec.add())
            v.tensor_sub(bc_rstd[:, :], bc_rstd[:, :], bc_nmr[:, :]).then_inc(vec, 1)
            c_vec.add()
            assert c_vec.v == E_VEC_VAR
            v.wait_ge(sce, E_SCE_EXP)
            v.scalar_tensor_tensor(bc_nmr[:, :], bc_m[:, :], cc(C_NONE), bc_rstd[:, :],
                                   ALU.mult, ALU.mult).then_inc(vec, 1)
            c_vec.add()
            assert c_vec.v == E_VEC_NMR
            for j in range(0, 32, 2):
                v.wait_ge(sy2r[j % 3], 16 * (j // 3 + 1))
                if j >= 3:
                    v.wait_ge(_st_sem(j - 3), _st_cnt(j - 3))
                v.tensor_mul(oslot[j % 3][:, :], y2r[j % 3][:, :],
                             bc_rstd[:, :]).then_inc(vec, 1)
                assert c_vec.add() == E_VEC_A1(j)
                v.wait_ge(vec, E_VEC_A1(j))
                v.tensor_add(oslot[j % 3][:, :], oslot[j % 3][:, :],
                             bc_nmr[:, :]).then_inc(vec, 1)
                assert c_vec.add() == E_VEC_A2(j)

        def phE_scalar(_e):
            _ = None
            s = _e
            s.wait_ge(vec, E_VEC_VAR)
            s.activation(bc_rstd[:, :], bc_rstd[:, :], AF.Ln,
                         bias=cc(C_EPS)).then_inc(scs, 1)
            s.wait_ge(scs, c_scs.add())
            s.activation(bc_rstd[:, :], bc_rstd[:, :], AF.Exp,
                         scale=cc(C_NHALF)).then_inc(sce, 1)
            c_sce.add()
            assert c_sce.v == E_SCE_EXP

            def _estore(k):
                s.wait_ge(vec, E_VEC_A2(k))
                s.dma_start(out_d[k * 128:(k + 1) * 128, :],
                            oslot[k % 3][:, :]).then_inc(sot[k % 3], 16)

            for j in range(0, 32, 2):
                s.wait_ge(sy2, 16 * (j + 1))
                if j >= 3:
                    s.wait_ge(spe, E_SPE_A1(j - 3))
                s.dma_start(y2r[j % 3][:, :],
                            y2d.ap()[j * 128:(j + 1) * 128, :]).then_inc(sy2r[j % 3], 16)
                if j >= 4:
                    _estore(j - 4)
            _estore(28)
            _estore(30)

        def phE_gpsimd(_e):
            _ = None
            g = _e
            g.wait_ge(sce, E_SCE_EXP)
            g.wait_ge(vec, E_VEC_NMR)
            for j in range(1, 32, 2):
                g.wait_ge(sy2r[j % 3], 16 * (j // 3 + 1))
                if j >= 3:
                    g.wait_ge(_st_sem(j - 3), _st_cnt(j - 3))
                g.tensor_mul(oslot[j % 3][:, :], y2r[j % 3][:, :],
                             bc_rstd[:, :]).then_inc(spe, 1)
                g.wait_ge(spe, E_SPE_A1(j))
                g.tensor_add(oslot[j % 3][:, :], oslot[j % 3][:, :],
                             bc_nmr[:, :]).then_inc(spe, 1)


    def _seq(*fns):
        def run(e):
            for f in fns:
                if f is not None:
                    f(e)
        return run


    with nc.Block() as blk:
        blk.vector(_seq(phB_vector, phC_vector, phD_vector, phE_vector))
        blk.tensor(_seq(phA_tensor, phB_tensor, phC_tensor, phD_tensor))
        blk.scalar(_seq(phA_scalar, phB_scalar, phC_scalar, phD_scalar, phE_scalar))
        blk.sync(_seq(phA_sync, phB_sync, phC_sync, phD_sync, phE_sync))
        blk.gpsimd(_seq(phA_gpsimd, phB_gpsimd, phD_gpsimd, phE_gpsimd))
    assert c_tmm.v == 213, c_tmm.v
    assert c_sce.v == 138, c_sce.v
    assert c_vec.v == 146, c_vec.v
    assert c_scs.v == 17, c_scs.v

    ctx.close()
    return nc


def _host_layouts(L0_attn_x, L0_attn_Wqkv, L0_attn_bqkv, L0_attn_Wo, L0_attn_bo,
                  L0_ffn_base_W, L0_ffn_router_W, L0_ffn_lora_A, L0_ffn_lora_B):
    x = np.asarray(L0_attn_x, dtype=np.float32)
    Wqkv = np.asarray(L0_attn_Wqkv, dtype=np.float32)
    bqkv = np.asarray(L0_attn_bqkv, dtype=np.float32)
    Wo = np.asarray(L0_attn_Wo, dtype=np.float32)
    bo = np.asarray(L0_attn_bo, dtype=np.float32)
    bW = np.asarray(L0_ffn_base_W, dtype=np.float32)
    rW = np.asarray(L0_ffn_router_W, dtype=np.float32)
    lA = np.asarray(L0_ffn_lora_A, dtype=np.float32)
    lB = np.asarray(L0_ffn_lora_B, dtype=np.float32)

    wq3 = Wqkv.reshape(32, 128, 3072).transpose(1, 0, 2)
    wq_host = np.ascontiguousarray(
        np.stack([wq3[:, :, s * SLF:(s + 1) * SLF].reshape(128, 32 * SLF)
                  for s in range(12)], axis=0))
    wo_host = np.ascontiguousarray(
        Wo.reshape(8, 128, 4096).transpose(1, 0, 2).reshape(128, 32768))
    BT = np.ascontiguousarray(bW.T)
    bw_host = np.ascontiguousarray(
        np.stack([BT.reshape(32, 128, 4096)[:, :, f * 128:(f + 1) * 128]
                  .transpose(1, 0, 2).reshape(128, 4096) for f in range(32)], axis=0))
    A2 = lA.transpose(1, 0, 2).reshape(4096, 128)
    a2_host = np.ascontiguousarray(
        A2.reshape(32, 128, 128).transpose(1, 0, 2).reshape(128, 4096))
    b2_host = np.ascontiguousarray(lB.reshape(128, 4096))
    rwt_host = np.ascontiguousarray(
        rW.T.reshape(32, 128, 8).transpose(1, 0, 2).reshape(128, 256))
    bqkv_host = np.ascontiguousarray(bqkv[None, :])
    bo_host = np.ascontiguousarray(bo[None, :])
    ident_host = np.eye(128, dtype=np.float32)

    in_maps = []
    for core in range(8):
        seq, q0 = core // 4, (core % 4) * 1024
        xt = np.ascontiguousarray(x[seq].T)
        xfm_host = np.ascontiguousarray(
            xt[:, q0:q0 + 1024].reshape(32, 128, 1024)
            .transpose(1, 0, 2).reshape(128, 32768))
        xtok_host = np.ascontiguousarray(x[seq, q0:q0 + 1024, :])
        in_maps.append({
            "xfm": xfm_host, "xtok": xtok_host, "wq": wq_host, "wo": wo_host,
            "bw": bw_host, "a2": a2_host, "b2": b2_host, "rwt": rwt_host,
            "bqkv": bqkv_host, "bo": bo_host, "ident": ident_host,
        })
    return in_maps


def kernel(**inputs):
    in_maps = _host_layouts(**inputs)
    nc = build_nc(debug=bool(int(os.environ.get("MK_DEBUG", "0"))))
    res = run_bass_kernel_spmd(nc, in_maps, core_ids=list(range(8)))
    out = np.empty((2, 4096, 4096), dtype=np.float32)
    for core in range(8):
        seq, q0 = core // 4, (core % 4) * 1024
        out[seq, q0:q0 + 1024, :] = np.asarray(res.results[core]["out"],
                                               dtype=np.float32).T
    return out


# version 22
# speedup vs baseline: 1.0047x; 1.0047x over previous
import os
import contextlib
import numpy as np
import ml_dtypes
import concourse.bass as bass
import concourse.mybir as mybir
from concourse.bass_utils import run_bass_kernel_spmd

F32 = mybir.dt.float32
BF16 = mybir.dt.bfloat16
AF = mybir.ActivationFunctionType
ALU = mybir.AluOpType
AX = mybir.AxisListType

NCH = 32
NT = 8
SLF = 256
KV_SLICES = [4, 5, 6, 7, 8, 9, 10, 11]
Q_SLICES = [0, 1, 2, 3]

EPS = 1e-5
SCALING = 2.0
INV32 = 1.0 / 32.0
BIG = 1e30


class Ctr:
    def __init__(self):
        self.v = 0

    def add(self, n=1):
        self.v += n
        return self.v


def build_nc(debug=False):
    nc = bass.Bass("TRN2", target_bir_lowering=False)

    xfm_d = nc.dram_tensor("xfm", [128, 32768], F32, kind="ExternalInput").ap()
    xtok_d = nc.dram_tensor("xtok", [1024, 4096], F32, kind="ExternalInput").ap()
    wq_d = nc.dram_tensor("wq", [12, 128, 8192], F32, kind="ExternalInput").ap()
    wo_d = nc.dram_tensor("wo", [128, 32768], F32, kind="ExternalInput").ap()
    bw_d = nc.dram_tensor("bw", [32, 128, 4096], F32, kind="ExternalInput").ap()
    a2_d = nc.dram_tensor("a2", [128, 4096], F32, kind="ExternalInput").ap()
    b2_d = nc.dram_tensor("b2", [128, 4096], F32, kind="ExternalInput").ap()
    rwt_d = nc.dram_tensor("rwt", [128, 256], F32, kind="ExternalInput").ap()
    bqkv_d = nc.dram_tensor("bqkv", [1, 3072], F32, kind="ExternalInput").ap()
    bo_d = nc.dram_tensor("bo", [1, 4096], F32, kind="ExternalInput").ap()
    ident_d = nc.dram_tensor("ident", [128, 128], F32, kind="ExternalInput").ap()
    out_d = nc.dram_tensor("out", [4096, 1024], F32, kind="ExternalOutput").ap()

    ikind = "ExternalOutput" if debug else "Internal"
    kcc_i = nc.dram_tensor("kcc_i", [1024, 1024], BF16, kind="Internal")
    kfmi = nc.dram_tensor("kfmi", [1024, 1024], BF16, kind="Internal")
    kcc_o = nc.dram_tensor("kcc_o", [4096, 1024], BF16, kind="Internal")
    vcc_i = nc.dram_tensor("vcc_i", [1024, 1024], BF16, kind="Internal")
    vcc_o = nc.dram_tensor("vcc_o", [4096, 1024], BF16, kind="Internal")
    kdump = nc.dram_tensor("kdump", [4096, 1024], BF16, kind=ikind) if debug else None
    vdump = nc.dram_tensor("vdump", [4096, 1024], BF16, kind=ikind) if debug else None
    qdump = nc.dram_tensor("qd", [128, 8192], BF16, kind=ikind) if debug else None
    aod = nc.dram_tensor("aod", [128, 8192], BF16, kind=ikind) if debug else None
    pd = nc.dram_tensor("pd", [1024, 4096], BF16, kind=ikind)
    sd = nc.dram_tensor("sd", [1024, 4096], BF16, kind=ikind)
    y2d = nc.dram_tensor("y2d", [4096, 1024], BF16, kind=ikind)
    wtd = nc.dram_tensor("wtd", [1024, 128], BF16, kind="Internal")

    BASE = (nc.sbuf_base + 31) & ~31
    TOP = 229344

    def sb(name, shape, dt, off):
        end = BASE + off + shape[1] * (2 if dt == BF16 else 4)
        assert end <= TOP, (name, end)
        return nc.alloc_sbuf_tensor_at(name, shape, dt, offset=BASE + off)

    consts = sb("consts", [128, 16], F32, 0)
    st = sb("st", [128, 32], F32, 64)
    dsum_tab = sb("dsum_tab", [128, 8], F32, 192)
    dinv_tab = sb("dinv_tab", [128, 8], F32, 224)
    dsumr = sb("dsumr", [1, 1024], BF16, 256)
    ones_r = sb("ones_r", [1, 1024], BF16, 320)
    ident = sb("ident_s", [128, 128], F32, 2368)
    ones16 = sb("ones16", [128, 16], F32, 2880)
    bqkv_r = sb("bqkv_r", [1, 3072], BF16, 2944)
    bo_r = sb("bo_r", [1, 4096], BF16, 9088)
    R1, R2, R3, RBW0 = 17280, 82816, 148352, 204608
    xfm = sb("xfm_s", [128, 32768], BF16, R1)
    kfm = sb("kfm", [128, 32768], BF16, R1)
    wo_s = sb("wo_s", [128, 32768], BF16, R1)
    sfm = sb("sfm", [128, 32768], BF16, R1)
    sqd2 = sb("sqd2", [128, 4096], BF16, R1)
    kvtok = sb("kvtok", [128, 16384], BF16, R2)
    kfmo = sb("kfmo", [128, 8192], BF16, R2 + 32768)
    vsb = sb("vsb", [128, 32768], BF16, R2)
    xtok_t = [sb(f"xtok{i}", [128, 4096], F32, R2 + i * 16384) for i in range(2)]
    y1_t = [sb(f"y1_{i}", [128, 4096], BF16, R2 + 32768 + i * 8192) for i in range(2)]
    sqd = sb("sqd", [128, 4096], BF16, R2 + 49152)
    bw_s = [sb("bw0", [128, 4096], BF16, RBW0),
            sb("bw1", [128, 4096], BF16, R2 + 0)]
    sqsl = [sb(f"sqsl{i}", [128, 1024], BF16, R2 + 8192 + i * 2048) for i in range(2)]
    srow = sb("srow", [1, 1024], F32, R2 + 12288)
    sqrow = sb("sqrow", [1, 1024], F32, R2 + 16384)
    bc_m = sb("bc_m", [128, 1024], F32, R2 + 32768)
    bc_rstd = sb("bc_rstd", [128, 1024], F32, R2 + 36864)
    bc_nmr = sb("bc_nmr", [128, 1024], F32, R2 + 40960)
    y2r = [sb(f"y2r{i}", [128, 1024], BF16, R2 + 45056 + i * 2048) for i in range(3)]
    oslot = ([sb(f"oslot{i}", [128, 1024], F32, R2 + 51200 + i * 4096) for i in range(3)]
             + [sb(f"oslot{i + 3}", [128, 1024], F32, R2 + i * 4096) for i in range(3)])
    ones_rf = sb("ones_rf", [1, 128], F32, R3 + 45312)
    ones_c = sb("ones_c", [128, 1], BF16, R3 + 45824)
    wq_s = [sb(f"wq{i}", [128, 8192], BF16, R3 + i * 16384) for i in range(2)]
    qfm = sb("qfm", [128, 8192], BF16, R3 + 32768)
    ptb = [sb(f"ptb{i}", [128, 512], BF16, R3 + i * 1024) for i in range(4)]
    ao = sb("ao", [128, 8192], BF16, R3 + 4096)
    P = sb("P", [128, 4096], BF16, R3 + 20480)
    s_t = [sb(f"s_{i}", [128, 4096], BF16, R3 + 32768 + i * 8192) for i in range(2)]
    b2_s = sb("b2_s", [128, 4096], BF16, R3 + 20480)
    rwt_s = sb("rwt_s", [128, 256], BF16, R3 + 28672)
    logit_s = sb("logit_s", [8, 1024], F32, R3 + 29184)
    ltok = sb("ltok", [128, 64], F32, R3 + 33280)
    tk = sb("tk", [128, 64], F32, R3 + 33536)
    tk2 = sb("tk2", [128, 320], F32, R3 + 33792)
    wet8 = sb("wet8", [128, 1024], BF16, R3 + 35072)
    wfm = sb("wfm", [128, 1024], BF16, R3 + 37120)
    midw = sb("midw", [128, 1024], BF16, R3 + 39168)
    y2_t = [sb(f"y2_{i}", [128, 1024], BF16, R3 + 41216 + i * 2048) for i in range(2)]
    a2_s = sb("a2_s", [128, 4096], BF16, R3 + 48064)

    C_EPS, C_NHALF, C_I32, C_ONE, C_NONE, C_SCL, C_NBIG, C_I4096, C_NI4096, C_ZERO, C_NB8 = range(11)
    CONST_VALS = [EPS, -0.5, INV32, 1.0, -1.0, SCALING, -BIG, 1.0 / 4096, -1.0 / 4096, 0.0, -8.0]

    def cc(i):
        return consts[:, i:i + 1]

    def sc_(j):
        return st[:, j:j + 1]

    def stc(qt, j):
        k = 6 + (qt % 2) * 6 + j
        return st[:, k:k + 1]

    ctx = contextlib.ExitStack()
    pm = ctx.enter_context(nc.psum_tensor("pm", [128, 4096], F32))

    def sem(name):
        return ctx.enter_context(nc.semaphore(name))

    sxf = [sem(f"sxf{i}") for i in range(4)]
    swq = [sem("swq0"), sem("swq1")]
    sbw = [sem("sbw0"), sem("sbw1")]
    swo = sem("swo")
    sab = sem("sab")
    gsm = sem("gsm")
    sdbg = sem("sdbg")
    skx, svx, skf, skfe = sem("skx"), sem("svx"), sem("skf"), sem("skfe")
    satk = [sem(f"satk{i}") for i in range(8)]
    svg = [sem(f"svg{i}") for i in range(8)]
    spt = [sem(f"spt{i}") for i in range(4)]
    spd = [sem("spd0"), sem("spd1")]
    sxt = [sem("sxt0"), sem("sxt1")]
    ssx = sem("ssx")
    ssf = [sem(f"ssf{i}") for i in range(4)]
    sy2 = sem("sy2")
    tst = sem("tst")
    sgc = sem("sgc")
    sy2r = [sem(f"sy2r{i}") for i in range(3)]
    sot = [sem(f"sot{i}") for i in range(3)]
    sog = [sem(f"sog{i}") for i in range(3)]
    spe = sem("spe")
    swt8 = sem("swt8")
    sa2 = sem("sa2")
    swt = sem("swt")
    ccs = sem("ccs")
    tmm = sem("tmm")
    sce = sem("sce")
    vec = sem("vec")
    scs = sem("scs")
    vss = sem("vss")

    c_tmm, c_sce, c_vec = Ctr(), Ctr(), Ctr()
    c_scs, c_vss = Ctr(), Ctr()

    A_pos = [list(range(NT)) if (si % 2 == 0) else list(range(NT - 1, -1, -1))
             for si in range(8)]

    def A_tmm(si, t):
        return si * NT + A_pos[si].index(t) + 1

    A_sce = A_tmm
    A_KV_END = 64

    def A_q(d):
        return A_KV_END + d + 1

    A_TMM_END = 72
    CCS_K, CCS_V = 1, 2

    if True:
        def phA_gpsimd(_e):
            _ = None
            g = _e
            for i, v in enumerate(CONST_VALS):
                g.memset(consts[:, i:i + 1], v)
            g.memset(ones_r[:, :], 1.0)
            g.memset(ones16[:, :], 1.0)
            g.dma_start(ident[:, :], ident_d[:, :]).then_inc(gsm, 16)
            g.dma_start(bqkv_r[:, :], bqkv_d[:, :]).then_inc(gsm, 16)
            g.dma_start(bo_r[:, :], bo_d[:, :]).then_inc(gsm, 16)
            for q4 in range(4):
                g.dma_start(xfm[:, q4 * 8192:(q4 + 1) * 8192],
                            xfm_d[:, q4 * 8192:(q4 + 1) * 8192]).then_inc(sxf[q4], 16)
            def wqload(i, s):
                if i >= 2:
                    j = i - 2
                    if j < 8:
                        g.wait_ge(tmm, (j + 1) * NT)
                    else:
                        g.wait_ge(tmm, A_KV_END + (j - 8) * 2 + 2)
                g.dma_start(wq_s[i % 2][:, :], wq_d[s, :, :]).then_inc(swq[i % 2], 16)

            allsl = KV_SLICES + Q_SLICES
            for i in range(6):
                wqload(i, allsl[i])
            g.wait_ge(skfe, 128)
            g.collective_compute(
                "AllGather", ALU.bypass,
                replica_groups=[[0, 1, 2, 3], [4, 5, 6, 7]],
                ins=[kfmi.ap()], outs=[kcc_o.ap()]).then_inc(ccs, 1)
            for i in range(6, 10):
                wqload(i, allsl[i])
            for i in range(10, 12):
                wqload(i, allsl[i])
            g.wait_ge(svx, 128)
            g.collective_compute(
                "AllGather", ALU.bypass,
                replica_groups=[[0, 1, 2, 3], [4, 5, 6, 7]],
                ins=[vcc_i.ap()], outs=[vcc_o.ap()]).then_inc(ccs, 1)
            g.dma_start(bw_s[0][:, :], bw_d[0, :, :]).then_inc(sbw[0], 16)

        def phA_tensor(_e):
            _ = None
            t = _e
            t.wait_ge(gsm, 48)
            for si, s in enumerate(KV_SLICES):
                t.wait_ge(swq[si % 2], 16 * (si // 2 + 1))
                F0 = s * SLF
                for ti in A_pos[si]:
                    if si == 0:
                        t.wait_ge(sxf[0], 16)
                    if si >= 1:
                        t.wait_ge(sce, A_sce(si - 1, ti))
                    ps = pm[:, ti * 512: ti * 512 + SLF]
                    for c in range(NCH):
                        if si == 0 and c % 8 == 0 and c > 0:
                            t.wait_ge(sxf[c // 8], 16)
                        t.matmul(ps,
                                 xfm[:, c * 1024 + ti * 128: c * 1024 + (ti + 1) * 128],
                                 wq_s[si % 2][:, c * SLF: (c + 1) * SLF],
                                 start=(c == 0), stop=False)
                    t.matmul(ps, ones_r[0:1, 0:128], bqkv_r[0:1, F0:F0 + SLF],
                             start=False, stop=True).then_inc(tmm, 1)
                    c_tmm.add()
            t.wait_ge(sce, A_KV_END)
            for qi, s in enumerate(Q_SLICES):
                si = 8 + qi
                t.wait_ge(swq[si % 2], 16 * (si // 2 + 1))
                for dsub in range(2):
                    d = qi * 2 + dsub
                    if d >= 4:
                        t.wait_ge(sce, A_q(d - 4))
                    for tq in range(2):
                        ps = pm[:, (d % 4) * 1024 + tq * 512: (d % 4) * 1024 + (tq + 1) * 512]
                        for c in range(NCH):
                            t.matmul(ps,
                                     wq_s[si % 2][:, c * SLF + dsub * 128:
                                                  c * SLF + dsub * 128 + 128],
                                     xfm[:, c * 1024 + tq * 512: c * 1024 + (tq + 1) * 512],
                                     start=(c == 0), stop=False)
                        ins = t.matmul(ps,
                                       bqkv_r[0:1, s * SLF + dsub * 128:
                                              s * SLF + dsub * 128 + 128],
                                       ones_r[0:1, tq * 512:(tq + 1) * 512],
                                       start=False, stop=True)
                        if tq == 1:
                            ins.then_inc(tmm, 1)
                            c_tmm.add()

        def phA_scalar(_e):
            _ = None
            s = _e
            for si, sl in enumerate(KV_SLICES):
                F0 = sl * SLF - 1024
                for ti in A_pos[si]:
                    s.wait_ge(tmm, A_tmm(si, ti))
                    s.activation(kvtok[:, ti * 2048 + F0: ti * 2048 + F0 + SLF],
                                 pm[:, ti * 512: ti * 512 + SLF],
                                 AF.Copy).then_inc(sce, 1)
                    c_sce.add()
            for d in range(8):
                s.wait_ge(tmm, A_q(d))
                s.activation(qfm[:, d * 1024:(d + 1) * 1024],
                             pm[:, (d % 4) * 1024:(d % 4 + 1) * 1024],
                             AF.Copy).then_inc(sce, 1)
                c_sce.add()
            s.wait_ge(ccs, CCS_K)
            s.wait_ge(tmm, A_TMM_END)
            for d in range(8):
                for rk in range(1, 4, 2):
                    s.dma_start(kfm[:, d * 4096 + rk * 1024: d * 4096 + (rk + 1) * 1024],
                                kcc_o.ap()[rk * 1024 + d * 128: rk * 1024 + (d + 1) * 128, :]
                                ).then_inc(satk[d], 16)

        def phA_sync(_e):
            _ = None
            s = _e
            for ti in range(NT):
                s.wait_ge(sce, A_sce(3, ti))
                s.dma_start(kcc_i.ap()[ti * 128:(ti + 1) * 128, :],
                            kvtok[:, ti * 2048: ti * 2048 + 1024]).then_inc(skx, 16)
            s.wait_ge(skx, 128)
            for d in range(8):
                s.dma_start(kfmo[:, d * 1024:(d + 1) * 1024],
                            kcc_i.ap()[0:1024, d * 128:(d + 1) * 128],
                            transpose=True).then_inc(skf, 16)
            s.wait_ge(skf, 128)
            for d in range(8):
                s.dma_start(kfmi.ap()[d * 128:(d + 1) * 128, :],
                            kfmo[:, d * 1024:(d + 1) * 1024]).then_inc(skfe, 16)
            for ti in range(NT):
                s.wait_ge(sce, A_sce(7, ti))
                s.dma_start(vcc_i.ap()[ti * 128:(ti + 1) * 128, :],
                            kvtok[:, ti * 2048 + 1024: ti * 2048 + 2048]).then_inc(svx, 16)
            s.wait_ge(ccs, CCS_K)
            s.wait_ge(tmm, A_TMM_END)
            for d in range(8):
                for rk in range(0, 4, 2):
                    s.dma_start(kfm[:, d * 4096 + rk * 1024: d * 4096 + (rk + 1) * 1024],
                                kcc_o.ap()[rk * 1024 + d * 128: rk * 1024 + (d + 1) * 128, :]
                                ).then_inc(satk[d], 16)

    B_tmm0 = 72

    def B_tmm_S(qt, kh):
        return B_tmm0 + 2 * qt + kh + 1

    def B_tmm_PV(half, kc):
        return B_tmm0 + 16 + half * 32 + kc + 1

    B_sce0 = 72

    def B_sce_exp(qt, kh):
        return B_sce0 + 2 * qt + kh + 1

    B_vec0 = 0

    def B_vec_dinv(qt):
        return B_vec0 + qt + 1

    def B_vec_pv(half, d):
        return B_vec0 + 8 + half * 8 + d + 1

    def PT_TGT(half, kc):
        return 16 * (half * 8 + kc // 4 + 1)

    if True:
        def phB_sync(_e):
            _ = None
            s = _e
            for qt in range(4):
                for kh in range(2):
                    s.wait_ge(sce, B_sce_exp(qt, kh))
                    s.dma_start(pd.ap()[qt * 128:(qt + 1) * 128,
                                        kh * 2048:(kh + 1) * 2048],
                                P[:, kh * 2048:(kh + 1) * 2048]).then_inc(spd[kh], 16)
            s.wait_ge(spd[0], 64)
            for kc in range(4):
                s.dma_start(ptb[kc % 4][:, :],
                            pd.ap()[0:512, kc * 128:(kc + 1) * 128],
                            transpose=True).then_inc(spt[kc % 4], 16)
            s.wait_ge(ccs, CCS_V)
            s.wait_ge(svx, 128)
            s.wait_ge(skfe, 128)
            for g in range(0, 8, 2):
                for kc in range(4 * g, 4 * g + 4):
                    s.dma_start(vsb[:, kc * 1024:(kc + 1) * 1024],
                                vcc_o.ap()[kc * 128:(kc + 1) * 128, :]).then_inc(svg[g], 16)
            for qt in range(4, NT):
                for kh in range(2):
                    s.wait_ge(sce, B_sce_exp(qt, kh))
                    s.dma_start(pd.ap()[qt * 128:(qt + 1) * 128,
                                        kh * 2048:(kh + 1) * 2048],
                                P[:, kh * 2048:(kh + 1) * 2048]).then_inc(spd[kh], 16)
            for half in range(2):
                for kc in range(4 if half == 0 else 0, 32):
                    idx = half * 32 + kc
                    if idx >= 4:
                        pi = idx - 4
                        s.wait_ge(tmm, B_tmm_PV(pi // 32, pi % 32))
                    s.wait_ge(spd[kc // 16], 64 if half == 0 else 128)
                    s.dma_start(ptb[kc % 4][:, :],
                                pd.ap()[half * 512:(half + 1) * 512,
                                        kc * 128:(kc + 1) * 128],
                                transpose=True).then_inc(spt[kc % 4], 16)

        def phB_scalar(_e):
            _ = None
            s = _e
            for qt in range(NT):
                s.wait_ge(tmm, B_tmm_S(qt, 0))
                if qt >= 1:
                    s.wait_ge(spd[0], 16 * qt)
                    s.wait_ge(vec, B_vec_dinv(qt - 1))
                s.activation(P[:, 0:2048], pm[:, 0:2048], AF.Exp,
                             bias=cc(C_NB8), scale=cc(C_I32),
                             accum_out=sc_(2)).then_inc(sce, 1)
                c_sce.add()
                s.wait_ge(tmm, B_tmm_S(qt, 1))
                if qt >= 1:
                    s.wait_ge(spd[1], 16 * qt)
                s.activation(P[:, 2048:4096], pm[:, 2048:4096], AF.Exp,
                             bias=cc(C_NB8), scale=cc(C_I32),
                             accum_out=sc_(5)).then_inc(sce, 1)
                c_sce.add()
            s.wait_ge(ccs, CCS_V)
            s.wait_ge(skx, 128)
            s.wait_ge(svx, 128)
            s.wait_ge(skfe, 128)
            for g in range(1, 8, 2):
                for kc in range(4 * g, 4 * g + 4):
                    s.dma_start(vsb[:, kc * 1024:(kc + 1) * 1024],
                                vcc_o.ap()[kc * 128:(kc + 1) * 128, :]).then_inc(svg[g], 16)

        def phB_gpsimd(_e):
            _ = None
            g = _e
            g.wait_ge(tmm, B_tmm_S(7, 1))
            _wo_gate = [None, (0, 8), (0, 24), (1, 8)]
            for i in range(4):
                if i >= 1:
                    g.wait_ge(tmm, B_tmm_PV(*_wo_gate[i]))
                g.dma_start(wo_s[:, i * 8192:(i + 1) * 8192],
                            wo_d[:, i * 8192:(i + 1) * 8192]).then_inc(swo, 16)
            g.wait_ge(tmm, B_tmm_PV(1, 31))
            g.dma_start(b2_s[:, :], b2_d[:, :]).then_inc(sab, 16)
            g.dma_start(rwt_s[:, :], rwt_d[:, :]).then_inc(sab, 16)
            if debug:
                g.dma_start(kdump.ap()[:, :], kcc_o.ap()[:, :]).then_inc(sdbg, 16)
                g.dma_start(vdump.ap()[:, :], vcc_o.ap()[:, :]).then_inc(sdbg, 16)
                g.dma_start(qdump.ap()[:, :], qfm[:, :]).then_inc(sdbg, 16)
                g.wait_ge(vec, B_vec_pv(1, 7))
                g.dma_start(aod.ap()[:, :], ao[:, :]).then_inc(sdbg, 16)
                g.wait_ge(sdbg, 64)

        def phB_tensor(_e):
            _ = None
            t = _e
            t.wait_ge(sce, 72)
            for qt in range(NT):
                for kh in range(2):
                    if qt >= 1:
                        t.wait_ge(sce, B_sce_exp(qt - 1, kh))
                    for d in range(8):
                        if qt == 0 and kh == 0:
                            t.wait_ge(satk[d], 64)
                        lhs = qfm[:, d * 1024 + qt * 128: d * 1024 + (qt + 1) * 128]
                        for kb in range(4 * kh, 4 * kh + 4):
                            ins = t.matmul(
                                pm[:, kb * 512:(kb + 1) * 512], lhs,
                                kfm[:, d * 4096 + kb * 512: d * 4096 + (kb + 1) * 512],
                                start=(d == 0), stop=(d == 7))
                            if d == 7 and kb == 4 * kh + 3:
                                ins.then_inc(tmm, 1)
                                c_tmm.add()
            t.wait_ge(sce, B_sce_exp(7, 1))
            for half in range(2):
                if half == 1:
                    for d in range(8):
                        t.wait_ge(vec, B_vec_pv(0, d))
                for kc in range(32):
                    if half == 0 and kc % 4 == 0:
                        t.wait_ge(svg[kc // 4], 64)
                    t.wait_ge(spt[kc % 4], PT_TGT(half, kc))
                    for d in range(8):
                        ins = t.matmul(
                            pm[:, d * 512:(d + 1) * 512],
                            vsb[:, kc * 1024 + d * 128: kc * 1024 + (d + 1) * 128],
                            ptb[kc % 4][:, :],
                            start=(kc == 0), stop=(kc == 31))
                        if d == 7:
                            ins.then_inc(tmm, 1)
                            c_tmm.add()

        def phB_vector(_e):
            _ = None
            v = _e
            for qt in range(NT):
                v.wait_ge(sce, B_sce_exp(qt, 1))
                v.tensor_add(dsum_tab[:, qt:qt + 1], sc_(2), sc_(5)).then_inc(vss, 1)
                v.wait_ge(vss, c_vss.add())
                v.reciprocal(dinv_tab[:, qt:qt + 1],
                             dsum_tab[:, qt:qt + 1]).then_inc(vec, 1)
                c_vec.add()
            for half in range(2):
                for d in range(8):
                    v.wait_ge(tmm, B_tmm_PV(half, 31))
                    v.tensor_copy(ao[:, d * 1024 + half * 512: d * 1024 + half * 512 + 512],
                                  pm[:, d * 512:(d + 1) * 512]).then_inc(vec, 1)
                    c_vec.add()

    C_tmm0 = 152

    def C_tmm(qt, half):
        return C_tmm0 + 1 + 2 * qt + half + 1

    C_vec0 = 24

    def C_vec_evac(qt, half):
        return C_vec0 + 1 + 5 * qt + half + 1

    def C_vec_var(qt):
        return C_vec0 + 1 + 5 * qt + 4

    def C_vec_nmr(qt):
        return C_vec0 + 1 + 5 * qt + 5

    C_sce0 = 88

    def C_sce_sq(qt):
        return C_sce0 + 2 * qt + 1

    def C_sce_apply(qt):
        return C_sce0 + 2 * qt + 2

    C_scs0 = 0

    def C_scs_rstd(qt):
        return C_scs0 + 2 * qt + 2

    if True:
        def phC_sync(_e):
            _ = None
            s = _e
            s.wait_ge(tmm, 152)
            for qt in range(2):
                s.dma_start(xtok_t[qt % 2][:, :],
                            xtok_d[qt * 128:(qt + 1) * 128, :]).then_inc(sxt[qt % 2], 16)
            for qt in range(NT):
                if qt + 2 < NT:
                    s.wait_ge(vec, C_vec_evac(qt, 1))
                    s.dma_start(xtok_t[qt % 2][:, :],
                                xtok_d[(qt + 2) * 128:(qt + 3) * 128, :]).then_inc(sxt[qt % 2], 16)
                s.wait_ge(sce, C_sce_apply(qt))
                if qt >= 1:
                    s.wait_ge(ssx, 16 * qt)
                s.dma_start(sd.ap()[qt * 128:(qt + 1) * 128, :],
                            s_t[qt % 2][:, :]).then_inc(ssx, 16)

        def phC_tensor(_e):
            _ = None
            t = _e
            t.wait_ge(swo, 64)
            t.wait_ge(vec, B_vec0 + 24)
            for j in range(NT):
                ins = t.transpose(pm[0:1, j * 128:(j + 1) * 128],
                                  dsum_tab[:, j:j + 1], ident[:, :])
                if j == NT - 1:
                    ins.then_inc(tmm, 1)
                    c_tmm.add()
            t.wait_ge(vec, C_vec0 + 1)
            for qt in range(NT):
                for half in range(2):
                    if qt >= 1:
                        t.wait_ge(vec, C_vec_evac(qt - 1, half))
                    for d in range(8):
                        lhs = ao[:, d * 1024 + qt * 128: d * 1024 + (qt + 1) * 128]
                        for cb in range(4):
                            col = half * 4 + cb
                            t.matmul(pm[:, col * 512:(col + 1) * 512], lhs,
                                     wo_s[:, d * 4096 + half * 2048 + cb * 512:
                                          d * 4096 + half * 2048 + (cb + 1) * 512],
                                     start=(d == 0), stop=False)
                    for cb in range(4):
                        col = half * 4 + cb
                        ins = t.matmul(pm[:, col * 512:(col + 1) * 512],
                                       dsumr[0:1, qt * 128:(qt + 1) * 128],
                                       bo_r[0:1, half * 2048 + cb * 512:
                                            half * 2048 + (cb + 1) * 512],
                                       start=False, stop=True)
                        if cb == 3:
                            ins.then_inc(tmm, 1)
                            c_tmm.add()

        def phC_vector(_e):
            _ = None
            v = _e
            v.wait_ge(tmm, C_tmm0 + 1)
            v.tensor_copy(dsumr[0:1, :], pm[0:1, 0:1024]).then_inc(vec, 1)
            c_vec.add()
            for qt in range(NT):
                for half in range(2):
                    v.wait_ge(tmm, C_tmm(qt, half))
                    v.wait_ge(sxt[qt % 2], 16 * (qt // 2 + 1))
                    if half == 0 and qt >= 2:
                        v.wait_ge(sce, C_sce_apply(qt - 2))
                    v.scalar_tensor_tensor(
                        y1_t[qt % 2][:, half * 2048:(half + 1) * 2048],
                        pm[:, half * 2048:(half + 1) * 2048],
                        dinv_tab[:, qt:qt + 1],
                        xtok_t[qt % 2][:, half * 2048:(half + 1) * 2048],
                        ALU.mult, ALU.add,
                        accum_out=stc(qt, half)).then_inc(vec, 1)
                    c_vec.add()
                v.wait_ge(vec, C_vec_evac(qt, 1))
                v.tensor_add(stc(qt, 2), stc(qt, 0), stc(qt, 1)).then_inc(vss, 1)
                v.wait_ge(vss, c_vss.add())
                v.tensor_scalar_mul(stc(qt, 2), stc(qt, 2), cc(C_NI4096)).then_inc(vec, 1)
                c_vec.add()
                v.wait_ge(vec, c_vec.v)
                v.wait_ge(sce, C_sce_sq(qt))
                v.tensor_scalar_mul(stc(qt, 4), stc(qt, 3), cc(C_I4096)).then_inc(vss, 1)
                v.wait_ge(vss, c_vss.add())
                v.tensor_mul(stc(qt, 5), stc(qt, 2), stc(qt, 2)).then_inc(vss, 1)
                v.wait_ge(vss, c_vss.add())
                v.tensor_sub(stc(qt, 4), stc(qt, 4), stc(qt, 5)).then_inc(vec, 1)
                c_vec.add()
                v.wait_ge(vec, c_vec.v)
                v.wait_ge(scs, C_scs_rstd(qt))
                v.tensor_mul(stc(qt, 5), stc(qt, 2), stc(qt, 4)).then_inc(vec, 1)
                c_vec.add()

        def phC_scalar(_e):
            _ = None
            s = _e
            if debug:
                s.wait_ge(sdbg, 64)
            for qt in range(NT):
                s.wait_ge(vec, C_vec_evac(qt, 1))
                s.activation(sqd[:, :], y1_t[qt % 2][:, :], AF.Square,
                             accum_out=stc(qt, 3)).then_inc(sce, 1)
                c_sce.add()
                s.wait_ge(vec, C_vec_var(qt))
                s.activation(stc(qt, 4), stc(qt, 4), AF.Ln, bias=cc(C_EPS)).then_inc(scs, 1)
                s.wait_ge(scs, c_scs.add())
                s.activation(stc(qt, 4), stc(qt, 4), AF.Exp,
                             scale=cc(C_NHALF)).then_inc(scs, 1)
                c_scs.add()
                s.wait_ge(vec, C_vec_nmr(qt))
                s.wait_ge(scs, C_scs_rstd(qt))
                if qt >= 2:
                    s.wait_ge(ssx, 16 * (qt - 1))
                s.activation(s_t[qt % 2][:, :], y1_t[qt % 2][:, :], AF.Identity,
                             bias=stc(qt, 5), scale=stc(qt, 4)).then_inc(sce, 1)
                c_sce.add()

    D_tmm0 = 169
    D_TMM_ROUTER = D_tmm0 + 1
    D_TMM_MID = D_tmm0 + 2

    def D_TMM_LT(i):
        return D_tmm0 + 2 + i + 1

    def D_TMM_FFN(fch):
        return D_tmm0 + 10 + fch + 1

    D_vec0 = 65
    D_VEC_REVAC = D_vec0 + 1

    def D_VEC_LTOK(i):
        return D_vec0 + 1 + i + 1

    D_VEC_MIDW = D_vec0 + 10

    def D_VEC_FFN(fch):
        return D_vec0 + 10 + fch + 1

    D_sce0 = 104
    D_SCE_TKEXP = D_sce0 + 1

    def D_SCE_SQ(j):
        return 106 + j

    TMM_BCS = 212
    TMM_BCQ = 213
    E_VEC_SROW = 108
    E_VEC_SQROW = 109
    E_VEC_VAR = 113
    E_VEC_NMR = 114
    E_SCE_EXP = 138

    def E_VEC_A1(j):
        assert j % 2 == 0
        return 115 + j

    def E_VEC_A2(j):
        assert j % 2 == 0
        return 116 + j

    def E_SPE_A1(j):
        assert j % 2 == 1
        return j

    def E_SPE_A2(j):
        assert j % 2 == 1
        return j + 1

    def _st_sem(j):
        return sot[j % 3] if j % 2 == 0 else sog[j % 3]

    def _st_cnt(j):
        return 16 * len([m for m in range(j % 3, j + 1, 3) if m % 2 == j % 2])

    def SFM_TGT(c):
        return 16 * (c // 4 + 1)

    tkm = {}

    if True:
        def phD_sync(_e):
            _ = None
            s = _e
            s.wait_ge(ssx, 128)
            for c in range(NCH):
                if c >= 4:
                    s.wait_ge(ssf[c % 4], SFM_TGT(c) - 16)
                s.dma_start(sfm[:, c * 1024:(c + 1) * 1024],
                            sd.ap()[0:1024, c * 128:(c + 1) * 128],
                            transpose=True).then_inc(ssf[c % 4], 16)
            s.wait_ge(swt8, 128)
            s.dma_start(wfm[:, :], wtd.ap()[0:1024, 0:128],
                        transpose=True).then_inc(swt, 16)
            for fch in range(32):
                s.wait_ge(vec, D_VEC_FFN(fch))
                if fch >= 1:
                    s.wait_ge(sy2, 16 * fch)
                s.dma_start(y2d.ap()[fch * 128:(fch + 1) * 128, :],
                            y2_t[fch % 2][:, :]).then_inc(sy2, 16)

        def phD_tensor(_e):
            _ = None
            t = _e
            t.wait_ge(sab, 32)
            for tq in range(2):
                for c in range(NCH):
                    t.wait_ge(ssf[c % 4], SFM_TGT(c))
                    ins = t.matmul(pm[0:8, 3072 + tq * 512: 3072 + (tq + 1) * 512],
                                   rwt_s[:, c * 8:(c + 1) * 8],
                                   sfm[:, c * 1024 + tq * 512: c * 1024 + tq * 512 + 512],
                                   start=(c == 0), stop=(c == NCH - 1))
                    if tq == 1 and c == NCH - 1:
                        ins.then_inc(tmm, 1)
                        c_tmm.add()
            t.wait_ge(sa2, 16)
            for tq in range(2):
                for c in range(NCH):
                    ins = t.matmul(pm[:, 2048 + tq * 512: 2048 + (tq + 1) * 512],
                                   a2_s[:, c * 128:(c + 1) * 128],
                                   sfm[:, c * 1024 + tq * 512: c * 1024 + tq * 512 + 512],
                                   start=(c == 0), stop=(c == NCH - 1))
                    if tq == 1 and c == NCH - 1:
                        ins.then_inc(tmm, 1)
                        c_tmm.add()
            t.wait_ge(vec, D_VEC_REVAC)
            for i in range(8):
                if i >= 1:
                    t.wait_ge(vec, D_VEC_LTOK(i - 1))
                t.transpose(pm[:, 3072 + i * 8: 3072 + (i + 1) * 8],
                            logit_s[0:8, i * 128:(i + 1) * 128],
                            ident[0:8, 0:8]).then_inc(tmm, 1)
                c_tmm.add()
            def _stats(j, last):
                t.wait_ge(sce, D_SCE_SQ(j))
                if j == 0:
                    t.wait_ge(sgc, 2)
                for tq in range(2):
                    t.matmul(pm[0:1, 2048 + tq * 512: 2048 + (tq + 1) * 512],
                             ones_c[:, :],
                             y2_t[j % 2][:, tq * 512:(tq + 1) * 512],
                             start=(j == 0), stop=(j == 31))
                for tq in range(2):
                    ins = t.matmul(pm[0:1, 3072 + tq * 512: 3072 + (tq + 1) * 512],
                                   ones_c[:, :],
                                   sqsl[j % 2][:, tq * 512:(tq + 1) * 512],
                                   start=(j == 0), stop=(j == 31))
                    if last and tq == 1:
                        ins.then_inc(tst, 1)

            for fch in range(32):
                t.wait_ge(sbw[fch % 2], 16 * (fch // 2 + 1))
                if fch >= 2:
                    t.wait_ge(vec, D_VEC_FFN(fch - 2))
                for tq in range(2):
                    for c in range(NCH):
                        t.matmul(pm[:, (fch % 2) * 1024 + tq * 512:
                                     (fch % 2) * 1024 + (tq + 1) * 512],
                                 bw_s[fch % 2][:, c * 128:(c + 1) * 128],
                                 sfm[:, c * 1024 + tq * 512: c * 1024 + tq * 512 + 512],
                                 start=(c == 0), stop=False)
                if fch >= 2:
                    _stats(fch - 2, False)
                if fch == 0:
                    t.wait_ge(swt, 16)
                    t.wait_ge(vec, D_VEC_MIDW)
                for tq in range(2):
                    ins = t.matmul(pm[:, (fch % 2) * 1024 + tq * 512:
                                       (fch % 2) * 1024 + (tq + 1) * 512],
                                   b2_s[:, fch * 128:(fch + 1) * 128],
                                   midw[:, tq * 512:(tq + 1) * 512],
                                   start=False, stop=True)
                    if tq == 1:
                        ins.then_inc(tmm, 1)
                        c_tmm.add()
            _stats(30, False)
            _stats(31, True)
            t.wait_ge(vec, E_VEC_SQROW)
            t.wait_ge(vec, D_VEC_FFN(30))
            for h in range(2):
                ins = t.matmul(pm[:, h * 512:(h + 1) * 512], ones_rf[:, :],
                               srow[0:1, h * 512:(h + 1) * 512],
                               start=True, stop=True)
                if h == 1:
                    ins.then_inc(tmm, 1)
                    c_tmm.add()
            t.wait_ge(vec, D_VEC_FFN(31))
            for h in range(2):
                ins = t.matmul(pm[:, 1024 + h * 512: 1024 + (h + 1) * 512],
                               ones_rf[:, :],
                               sqrow[0:1, h * 512:(h + 1) * 512],
                               start=True, stop=True)
                if h == 1:
                    ins.then_inc(tmm, 1)
                    c_tmm.add()

        def phD_vector(_e):
            _ = None
            v = _e
            SH = tk2[:, 0:64]
            EQ = tk2[:, 64:128]
            W = tk2[:, 128:192]
            E = tk2[:, 192:256]
            v.wait_ge(tmm, D_TMM_ROUTER)
            v.tensor_copy(logit_s[0:8, :], pm[0:8, 3072:4096]).then_inc(vec, 1)
            c_vec.add()
            for i in range(8):
                v.wait_ge(tmm, D_TMM_LT(i))
                v.tensor_copy(ltok[:, i * 8:(i + 1) * 8],
                              pm[:, 3072 + i * 8: 3072 + (i + 1) * 8]).then_inc(vec, 1)
                c_vec.add()
            v.wait_ge(vec, c_vec.v)
            for i in range(8):
                v.reduce_max(tk[:, i:i + 1], ltok[:, i * 8:(i + 1) * 8],
                             axis=AX.X, negate=True).then_inc(vss, 1)
                c_vss.add()
            v.wait_ge(vss, c_vss.v)
            for i in range(8):
                v.tensor_scalar_add(SH[:, i * 8:(i + 1) * 8],
                                    ltok[:, i * 8:(i + 1) * 8],
                                    tk[:, i:i + 1]).then_inc(vss, 1)
                c_vss.add()
            v.wait_ge(vss, c_vss.v)
            v.tensor_scalar(EQ, SH, cc(C_ZERO), None, ALU.is_equal).then_inc(vss, 1)
            v.wait_ge(vss, c_vss.add())
            v.scalar_tensor_tensor(W, EQ, cc(C_NBIG), SH, ALU.mult, ALU.add).then_inc(vss, 1)
            v.wait_ge(vss, c_vss.add())
            for i in range(8):
                v.reduce_max(tk[:, 8 + i:9 + i], W[:, i * 8:(i + 1) * 8],
                             axis=AX.X).then_inc(vss, 1)
                c_vss.add()
            v.wait_ge(vss, c_vss.v)
            for i in range(8):
                v.tensor_scalar(EQ[:, i * 8:(i + 1) * 8], SH[:, i * 8:(i + 1) * 8],
                                tk[:, 8 + i:9 + i], None, ALU.is_ge).then_inc(vss, 1)
                c_vss.add()
            tkm["exp_in"] = c_vss.v
            v.wait_ge(sce, D_SCE_TKEXP)
            v.tensor_mul(W, E, EQ).then_inc(vss, 1)
            v.wait_ge(vss, c_vss.add())
            for i in range(8):
                v.reduce_sum(tk[:, 16 + i:17 + i], W[:, i * 8:(i + 1) * 8],
                             axis=AX.X).then_inc(vss, 1)
                c_vss.add()
            v.wait_ge(vss, c_vss.v)
            v.reciprocal(tk[:, 24:32], tk[:, 16:24]).then_inc(vss, 1)
            v.wait_ge(vss, c_vss.add())
            v.tensor_scalar_mul(tk[:, 24:32], tk[:, 24:32], cc(C_SCL)).then_inc(vss, 1)
            v.wait_ge(vss, c_vss.add())
            for i in range(8):
                v.tensor_scalar_mul(W[:, i * 8:(i + 1) * 8], W[:, i * 8:(i + 1) * 8],
                                    tk[:, 24 + i:25 + i]).then_inc(vss, 1)
                c_vss.add()
            v.wait_ge(vss, c_vss.v)
            for i in range(8):
                for e in range(8):
                    v.tensor_scalar_mul(wet8[:, i * 128 + e * 16: i * 128 + (e + 1) * 16],
                                        ones16[:, :],
                                        W[:, i * 8 + e: i * 8 + e + 1]).then_inc(vss, 1)
                    c_vss.add()
            tkm["wet8"] = c_vss.v
            v.wait_ge(swt, 16)
            v.wait_ge(tmm, D_TMM_MID)
            v.tensor_mul(midw[:, :], pm[:, 2048:3072], wfm[:, :]).then_inc(vec, 1)
            c_vec.add()
            assert c_vec.v == D_VEC_MIDW
            for fch in range(32):
                v.wait_ge(tmm, D_TMM_FFN(fch))
                if fch >= 2:
                    v.wait_ge(sy2, 16 * (fch - 1))
                v.scalar_tensor_tensor(
                    y2_t[fch % 2][:, :],
                    pm[:, (fch % 2) * 1024:(fch % 2 + 1) * 1024],
                    cc(C_ONE),
                    sfm[:, fch * 1024:(fch + 1) * 1024],
                    ALU.mult, ALU.add).then_inc(vec, 1)
                c_vec.add()

        def phD_gpsimd(_e):
            _ = None
            g = _e
            g.wait_ge(sce, C_sce_apply(7))
            g.wait_ge(ssx, 128)
            g.memset(ones_c[:, :], 1.0).then_inc(sgc, 1)
            g.memset(ones_rf[:, :], 1.0).then_inc(sgc, 1)
            g.dma_start(a2_s[:, :], a2_d[:, :]).then_inc(sa2, 16)
            g.dma_start(bw_s[1][:, :], bw_d[1, :, :]).then_inc(sbw[1], 16)
            g.wait_ge(vss, tkm["wet8"])
            for i in range(8):
                g.dma_start(wtd.ap()[i * 128:(i + 1) * 128, :],
                            wet8[:, i * 128:(i + 1) * 128]).then_inc(swt8, 16)
            for fch in range(2, 32):
                g.wait_ge(vec, D_VEC_FFN(fch - 2))
                g.dma_start(bw_s[fch % 2][:, :], bw_d[fch, :, :]).then_inc(sbw[fch % 2], 16)

        def phD_scalar(_e):
            _ = None
            s = _e
            s.wait_ge(vss, tkm["exp_in"])
            s.activation(tk2[:, 192:256], tk2[:, 0:64], AF.Exp).then_inc(sce, 1)
            c_sce.add()
            for j in range(32):
                s.wait_ge(vec, D_VEC_FFN(j))
                s.activation(sqsl[j % 2][:, :], y2_t[j % 2][:, :],
                             AF.Square).then_inc(sce, 1)
                c_sce.add()

    if True:
        def phE_sync(_e):
            _ = None
            s = _e

            def _ostore(k):
                s.wait_ge(spe, E_SPE_A2(k))
                s.dma_start(out_d[k * 128:(k + 1) * 128, :],
                            oslot[k % 6][:, :]).then_inc(sog[k % 3], 16)

            for j in range(1, 32, 2):
                s.wait_ge(sy2, 16 * (j + 1))
                if j >= 3:
                    s.wait_ge(vec, E_VEC_A1(j - 3))
                s.dma_start(y2r[j % 3][:, :],
                            y2d.ap()[j * 128:(j + 1) * 128, :]).then_inc(sy2r[j % 3], 16)
                if j >= 5:
                    _ostore(j - 4)
            _ostore(29)
            _ostore(31)
            for ss in range(3):
                s.wait_ge(sot[ss], _st_cnt(max(m for m in range(32) if m % 3 == ss and m % 2 == 0)))
                s.wait_ge(sog[ss], _st_cnt(max(m for m in range(32) if m % 3 == ss and m % 2 == 1)))

        def phE_vector(_e):
            _ = None
            v = _e
            v.wait_ge(tst, 1)
            v.tensor_copy(srow[0:1, :], pm[0:1, 2048:3072]).then_inc(vec, 1)
            c_vec.add()
            v.tensor_copy(sqrow[0:1, :], pm[0:1, 3072:4096]).then_inc(vec, 1)
            c_vec.add()
            assert c_vec.v == E_VEC_SQROW
            v.wait_ge(tmm, TMM_BCS)
            v.tensor_scalar_mul(bc_m[:, :], pm[:, 0:1024], cc(C_I4096)).then_inc(vec, 1)
            c_vec.add()
            v.wait_ge(tmm, TMM_BCQ)
            v.tensor_scalar_mul(bc_rstd[:, :], pm[:, 1024:2048],
                                cc(C_I4096)).then_inc(vec, 1)
            v.wait_ge(vec, c_vec.add())
            v.tensor_mul(bc_nmr[:, :], bc_m[:, :], bc_m[:, :]).then_inc(vec, 1)
            v.wait_ge(vec, c_vec.add())
            v.tensor_sub(bc_rstd[:, :], bc_rstd[:, :], bc_nmr[:, :]).then_inc(vec, 1)
            c_vec.add()
            assert c_vec.v == E_VEC_VAR
            v.wait_ge(sce, E_SCE_EXP)
            v.scalar_tensor_tensor(bc_nmr[:, :], bc_m[:, :], cc(C_NONE), bc_rstd[:, :],
                                   ALU.mult, ALU.mult).then_inc(vec, 1)
            c_vec.add()
            assert c_vec.v == E_VEC_NMR
            for j in range(0, 32, 2):
                v.wait_ge(sy2r[j % 3], 16 * (j // 3 + 1))
                if j >= 6:
                    v.wait_ge(_st_sem(j - 6), _st_cnt(j - 6))
                v.tensor_mul(oslot[j % 6][:, :], y2r[j % 3][:, :],
                             bc_rstd[:, :]).then_inc(vec, 1)
                assert c_vec.add() == E_VEC_A1(j)
                v.wait_ge(vec, E_VEC_A1(j))
                v.tensor_add(oslot[j % 6][:, :], oslot[j % 6][:, :],
                             bc_nmr[:, :]).then_inc(vec, 1)
                assert c_vec.add() == E_VEC_A2(j)

        def phE_scalar(_e):
            _ = None
            s = _e
            s.wait_ge(vec, E_VEC_VAR)
            s.activation(bc_rstd[:, :], bc_rstd[:, :], AF.Ln,
                         bias=cc(C_EPS)).then_inc(scs, 1)
            s.wait_ge(scs, c_scs.add())
            s.activation(bc_rstd[:, :], bc_rstd[:, :], AF.Exp,
                         scale=cc(C_NHALF)).then_inc(sce, 1)
            c_sce.add()
            assert c_sce.v == E_SCE_EXP

            def _estore(k):
                s.wait_ge(vec, E_VEC_A2(k))
                s.dma_start(out_d[k * 128:(k + 1) * 128, :],
                            oslot[k % 6][:, :]).then_inc(sot[k % 3], 16)

            for j in range(0, 32, 2):
                s.wait_ge(sy2, 16 * (j + 1))
                if j >= 3:
                    s.wait_ge(spe, E_SPE_A1(j - 3))
                s.dma_start(y2r[j % 3][:, :],
                            y2d.ap()[j * 128:(j + 1) * 128, :]).then_inc(sy2r[j % 3], 16)
                if j >= 4:
                    _estore(j - 4)
            _estore(28)
            _estore(30)

        def phE_gpsimd(_e):
            _ = None
            g = _e
            g.wait_ge(sce, E_SCE_EXP)
            g.wait_ge(vec, E_VEC_NMR)
            for j in range(1, 32, 2):
                g.wait_ge(sy2r[j % 3], 16 * (j // 3 + 1))
                if j >= 6:
                    g.wait_ge(_st_sem(j - 6), _st_cnt(j - 6))
                g.tensor_mul(oslot[j % 6][:, :], y2r[j % 3][:, :],
                             bc_rstd[:, :]).then_inc(spe, 1)
                g.wait_ge(spe, E_SPE_A1(j))
                g.tensor_add(oslot[j % 6][:, :], oslot[j % 6][:, :],
                             bc_nmr[:, :]).then_inc(spe, 1)


    def _seq(*fns):
        def run(e):
            for f in fns:
                if f is not None:
                    f(e)
        return run


    with nc.Block() as blk:
        blk.vector(_seq(phB_vector, phC_vector, phD_vector, phE_vector))
        blk.tensor(_seq(phA_tensor, phB_tensor, phC_tensor, phD_tensor))
        blk.scalar(_seq(phA_scalar, phB_scalar, phC_scalar, phD_scalar, phE_scalar))
        blk.sync(_seq(phA_sync, phB_sync, phC_sync, phD_sync, phE_sync))
        blk.gpsimd(_seq(phA_gpsimd, phB_gpsimd, phD_gpsimd, phE_gpsimd))
    assert c_tmm.v == 213, c_tmm.v
    assert c_sce.v == 138, c_sce.v
    assert c_vec.v == 146, c_vec.v
    assert c_scs.v == 17, c_scs.v

    ctx.close()
    return nc


def _host_layouts(L0_attn_x, L0_attn_Wqkv, L0_attn_bqkv, L0_attn_Wo, L0_attn_bo,
                  L0_ffn_base_W, L0_ffn_router_W, L0_ffn_lora_A, L0_ffn_lora_B):
    x = np.asarray(L0_attn_x, dtype=np.float32)
    Wqkv = np.asarray(L0_attn_Wqkv, dtype=np.float32)
    bqkv = np.asarray(L0_attn_bqkv, dtype=np.float32)
    Wo = np.asarray(L0_attn_Wo, dtype=np.float32)
    bo = np.asarray(L0_attn_bo, dtype=np.float32)
    bW = np.asarray(L0_ffn_base_W, dtype=np.float32)
    rW = np.asarray(L0_ffn_router_W, dtype=np.float32)
    lA = np.asarray(L0_ffn_lora_A, dtype=np.float32)
    lB = np.asarray(L0_ffn_lora_B, dtype=np.float32)

    wq3 = Wqkv.reshape(32, 128, 3072).transpose(1, 0, 2)
    wq_host = np.ascontiguousarray(
        np.stack([wq3[:, :, s * SLF:(s + 1) * SLF].reshape(128, 32 * SLF)
                  for s in range(12)], axis=0))
    wo_host = np.ascontiguousarray(
        Wo.reshape(8, 128, 4096).transpose(1, 0, 2).reshape(128, 32768))
    BT = np.ascontiguousarray(bW.T)
    bw_host = np.ascontiguousarray(
        np.stack([BT.reshape(32, 128, 4096)[:, :, f * 128:(f + 1) * 128]
                  .transpose(1, 0, 2).reshape(128, 4096) for f in range(32)], axis=0))
    A2 = lA.transpose(1, 0, 2).reshape(4096, 128)
    a2_host = np.ascontiguousarray(
        A2.reshape(32, 128, 128).transpose(1, 0, 2).reshape(128, 4096))
    b2_host = np.ascontiguousarray(lB.reshape(128, 4096))
    rwt_host = np.ascontiguousarray(
        rW.T.reshape(32, 128, 8).transpose(1, 0, 2).reshape(128, 256))
    bqkv_host = np.ascontiguousarray(bqkv[None, :])
    bo_host = np.ascontiguousarray(bo[None, :])
    ident_host = np.eye(128, dtype=np.float32)

    in_maps = []
    for core in range(8):
        seq, q0 = core // 4, (core % 4) * 1024
        xt = np.ascontiguousarray(x[seq].T)
        xfm_host = np.ascontiguousarray(
            xt[:, q0:q0 + 1024].reshape(32, 128, 1024)
            .transpose(1, 0, 2).reshape(128, 32768))
        xtok_host = np.ascontiguousarray(x[seq, q0:q0 + 1024, :])
        in_maps.append({
            "xfm": xfm_host, "xtok": xtok_host, "wq": wq_host, "wo": wo_host,
            "bw": bw_host, "a2": a2_host, "b2": b2_host, "rwt": rwt_host,
            "bqkv": bqkv_host, "bo": bo_host, "ident": ident_host,
        })
    return in_maps


def kernel(**inputs):
    in_maps = _host_layouts(**inputs)
    nc = build_nc(debug=bool(int(os.environ.get("MK_DEBUG", "0"))))
    res = run_bass_kernel_spmd(nc, in_maps, core_ids=list(range(8)))
    out = np.empty((2, 4096, 4096), dtype=np.float32)
    for core in range(8):
        seq, q0 = core // 4, (core % 4) * 1024
        out[seq, q0:q0 + 1024, :] = np.asarray(res.results[core]["out"],
                                               dtype=np.float32).T
    return out


# version 23 (final)
# speedup vs baseline: 1.0496x; 1.0447x over previous
import os
import contextlib
import numpy as np
import ml_dtypes
import concourse.bass as bass
import concourse.mybir as mybir
from concourse.bass_utils import run_bass_kernel_spmd

F32 = mybir.dt.float32
BF16 = mybir.dt.bfloat16
AF = mybir.ActivationFunctionType
ALU = mybir.AluOpType
AX = mybir.AxisListType

NCH = 32
NT = 8
SLF = 256
KV_SLICES = [4, 5, 6, 7, 8, 9, 10, 11]
Q_SLICES = [0, 1, 2, 3]

EPS = 1e-5
SCALING = 2.0
INV32 = 1.0 / 32.0
BIG = 1e30


class Ctr:
    def __init__(self):
        self.v = 0

    def add(self, n=1):
        self.v += n
        return self.v


def build_nc(debug=False):
    nc = bass.Bass("TRN2", target_bir_lowering=False)

    xfm_d = nc.dram_tensor("xfm", [128, 32768], F32, kind="ExternalInput").ap()
    xtok_d = nc.dram_tensor("xtok", [1024, 4096], F32, kind="ExternalInput").ap()
    wq_d = nc.dram_tensor("wq", [12, 128, 8192], F32, kind="ExternalInput").ap()
    wo_d = nc.dram_tensor("wo", [128, 32768], F32, kind="ExternalInput").ap()
    bw_d = nc.dram_tensor("bw", [32, 128, 4096], F32, kind="ExternalInput").ap()
    a2_d = nc.dram_tensor("a2", [128, 4096], F32, kind="ExternalInput").ap()
    b2_d = nc.dram_tensor("b2", [128, 4096], F32, kind="ExternalInput").ap()
    rwt_d = nc.dram_tensor("rwt", [128, 256], F32, kind="ExternalInput").ap()
    bqkv_d = nc.dram_tensor("bqkv", [1, 3072], F32, kind="ExternalInput").ap()
    bo_d = nc.dram_tensor("bo", [1, 4096], F32, kind="ExternalInput").ap()
    ident_d = nc.dram_tensor("ident", [128, 128], F32, kind="ExternalInput").ap()
    out_d = nc.dram_tensor("out", [1024, 4096], F32, kind="ExternalOutput").ap()

    ikind = "ExternalOutput" if debug else "Internal"
    kcc_i = nc.dram_tensor("kcc_i", [1024, 1024], BF16, kind="Internal")
    kfmi = nc.dram_tensor("kfmi", [1024, 1024], BF16, kind="Internal")
    kcc_o = nc.dram_tensor("kcc_o", [4096, 1024], BF16, kind="Internal")
    vcc_i = nc.dram_tensor("vcc_i", [1024, 1024], BF16, kind="Internal")
    vcc_o = nc.dram_tensor("vcc_o", [4096, 1024], BF16, kind="Internal")
    kdump = nc.dram_tensor("kdump", [4096, 1024], BF16, kind=ikind) if debug else None
    vdump = nc.dram_tensor("vdump", [4096, 1024], BF16, kind=ikind) if debug else None
    qdump = nc.dram_tensor("qd", [128, 8192], BF16, kind=ikind) if debug else None
    aod = nc.dram_tensor("aod", [128, 8192], BF16, kind=ikind) if debug else None
    pd = nc.dram_tensor("pd", [1024, 4096], BF16, kind=ikind)
    sd = nc.dram_tensor("sd", [1024, 4096], BF16, kind=ikind)
    y2d = nc.dram_tensor("y2d", [4096, 1024], BF16, kind=ikind)
    wtd = nc.dram_tensor("wtd", [1024, 128], BF16, kind="Internal")

    BASE = (nc.sbuf_base + 31) & ~31
    TOP = 229344

    def sb(name, shape, dt, off):
        end = BASE + off + shape[1] * (2 if dt == BF16 else 4)
        assert end <= TOP, (name, end)
        return nc.alloc_sbuf_tensor_at(name, shape, dt, offset=BASE + off)

    consts = sb("consts", [128, 16], F32, 0)
    st = sb("st", [128, 32], F32, 64)
    dsum_tab = sb("dsum_tab", [128, 8], F32, 192)
    dinv_tab = sb("dinv_tab", [128, 8], F32, 224)
    dsumr = sb("dsumr", [1, 1024], BF16, 256)
    ones_r = sb("ones_r", [1, 1024], BF16, 320)
    ident = sb("ident_s", [128, 128], F32, 2368)
    ones16 = sb("ones16", [128, 16], F32, 2880)
    bqkv_r = sb("bqkv_r", [1, 3072], BF16, 2944)
    bo_r = sb("bo_r", [1, 4096], BF16, 9088)
    R1, R2, R3, RBW0 = 17280, 82816, 148352, 204608
    xfm = sb("xfm_s", [128, 32768], BF16, R1)
    kfm = sb("kfm", [128, 32768], BF16, R1)
    wo_s = sb("wo_s", [128, 32768], BF16, R1)
    sfm = sb("sfm", [128, 32768], BF16, R1)
    sqd2 = sb("sqd2", [128, 4096], BF16, R1)
    kvtok = sb("kvtok", [128, 16384], BF16, R2)
    kfmo = sb("kfmo", [128, 8192], BF16, R2 + 32768)
    vsb = sb("vsb", [128, 32768], BF16, R2)
    xtok_t = [sb(f"xtok{i}", [128, 4096], F32, R2 + i * 16384) for i in range(2)]
    y1_t = [sb(f"y1_{i}", [128, 4096], BF16, R2 + 32768 + i * 8192) for i in range(2)]
    sqd = sb("sqd", [128, 4096], BF16, R2 + 49152)
    bw_s = [sb("bw0", [128, 4096], BF16, RBW0),
            sb("bw1", [128, 4096], BF16, R2 + 0)]
    y2tok_t = [sb(f"y2tok{i}", [128, 4096], BF16, R2 + i * 8192) for i in range(4)]
    outt = [sb(f"outt{i}", [128, 4096], F32, R2 + 32768 + i * 16384) for i in range(2)]
    wq_s = [sb(f"wq{i}", [128, 8192], BF16, R3 + i * 16384) for i in range(2)]
    qfm = sb("qfm", [128, 8192], BF16, R3 + 32768)
    ptb = [sb(f"ptb{i}", [128, 512], BF16, R3 + i * 1024) for i in range(4)]
    ao = sb("ao", [128, 8192], BF16, R3 + 4096)
    P = sb("P", [128, 4096], BF16, R3 + 20480)
    s_t = [sb(f"s_{i}", [128, 4096], BF16, R3 + 32768 + i * 8192) for i in range(2)]
    b2_s = sb("b2_s", [128, 4096], BF16, R3 + 20480)
    rwt_s = sb("rwt_s", [128, 256], BF16, R3 + 28672)
    logit_s = sb("logit_s", [8, 1024], F32, R3 + 29184)
    ltok = sb("ltok", [128, 64], F32, R3 + 33280)
    tk = sb("tk", [128, 64], F32, R3 + 33536)
    tk2 = sb("tk2", [128, 320], F32, R3 + 33792)
    wet8 = sb("wet8", [128, 1024], BF16, R3 + 35072)
    wfm = sb("wfm", [128, 1024], BF16, R3 + 37120)
    midw = sb("midw", [128, 1024], BF16, R3 + 39168)
    y2_t = [sb(f"y2_{i}", [128, 1024], BF16, R3 + 41216 + i * 2048) for i in range(2)]
    a2_s = sb("a2_s", [128, 4096], BF16, R3 + 48064)

    C_EPS, C_NHALF, C_I32, C_ONE, C_NONE, C_SCL, C_NBIG, C_I4096, C_NI4096, C_ZERO, C_NB8 = range(11)
    CONST_VALS = [EPS, -0.5, INV32, 1.0, -1.0, SCALING, -BIG, 1.0 / 4096, -1.0 / 4096, 0.0, -8.0]

    def cc(i):
        return consts[:, i:i + 1]

    def sc_(j):
        return st[:, j:j + 1]

    def stc(qt, j):
        k = 6 + (qt % 2) * 6 + j
        return st[:, k:k + 1]

    ctx = contextlib.ExitStack()
    pm = ctx.enter_context(nc.psum_tensor("pm", [128, 4096], F32))

    def sem(name):
        return ctx.enter_context(nc.semaphore(name))

    sxf = [sem(f"sxf{i}") for i in range(4)]
    swq = [sem("swq0"), sem("swq1")]
    sbw = [sem("sbw0"), sem("sbw1")]
    swo = sem("swo")
    sab = sem("sab")
    gsm = sem("gsm")
    sdbg = sem("sdbg")
    skx, svx, skf, skfe = sem("skx"), sem("svx"), sem("skf"), sem("skfe")
    satk = [sem(f"satk{i}") for i in range(8)]
    svg = [sem(f"svg{i}") for i in range(8)]
    spt = [sem(f"spt{i}") for i in range(4)]
    spd = [sem("spd0"), sem("spd1")]
    sxt = [sem("sxt0"), sem("sxt1")]
    ssx = sem("ssx")
    ssf = [sem(f"ssf{i}") for i in range(4)]
    sy2 = sem("sy2")
    sy2t = [sem(f"sy2t{i}") for i in range(4)]
    sot = [sem("sot0"), sem("sot1")]
    swt8 = sem("swt8")
    sa2 = sem("sa2")
    swt = sem("swt")
    ccs = sem("ccs")
    tmm = sem("tmm")
    sce = sem("sce")
    vec = sem("vec")
    scs = sem("scs")
    vss = sem("vss")

    c_tmm, c_sce, c_vec = Ctr(), Ctr(), Ctr()
    c_scs, c_vss = Ctr(), Ctr()

    A_pos = [list(range(NT)) if (si % 2 == 0) else list(range(NT - 1, -1, -1))
             for si in range(8)]

    def A_tmm(si, t):
        return si * NT + A_pos[si].index(t) + 1

    A_sce = A_tmm
    A_KV_END = 64

    def A_q(d):
        return A_KV_END + d + 1

    A_TMM_END = 72
    CCS_K, CCS_V = 1, 2

    if True:
        def phA_gpsimd(_e):
            _ = None
            g = _e
            for i, v in enumerate(CONST_VALS):
                g.memset(consts[:, i:i + 1], v)
            g.memset(ones_r[:, :], 1.0)
            g.memset(ones16[:, :], 1.0)
            g.dma_start(ident[:, :], ident_d[:, :]).then_inc(gsm, 16)
            g.dma_start(bqkv_r[:, :], bqkv_d[:, :]).then_inc(gsm, 16)
            g.dma_start(bo_r[:, :], bo_d[:, :]).then_inc(gsm, 16)
            for q4 in range(4):
                g.dma_start(xfm[:, q4 * 8192:(q4 + 1) * 8192],
                            xfm_d[:, q4 * 8192:(q4 + 1) * 8192]).then_inc(sxf[q4], 16)
            def wqload(i, s):
                if i >= 2:
                    j = i - 2
                    if j < 8:
                        g.wait_ge(tmm, (j + 1) * NT)
                    else:
                        g.wait_ge(tmm, A_KV_END + (j - 8) * 2 + 2)
                g.dma_start(wq_s[i % 2][:, :], wq_d[s, :, :]).then_inc(swq[i % 2], 16)

            allsl = KV_SLICES + Q_SLICES
            for i in range(6):
                wqload(i, allsl[i])
            g.wait_ge(skfe, 128)
            g.collective_compute(
                "AllGather", ALU.bypass,
                replica_groups=[[0, 1, 2, 3], [4, 5, 6, 7]],
                ins=[kfmi.ap()], outs=[kcc_o.ap()]).then_inc(ccs, 1)
            for i in range(6, 10):
                wqload(i, allsl[i])
            g.wait_ge(svx, 128)
            g.collective_compute(
                "AllGather", ALU.bypass,
                replica_groups=[[0, 1, 2, 3], [4, 5, 6, 7]],
                ins=[vcc_i.ap()], outs=[vcc_o.ap()]).then_inc(ccs, 1)
            for i in range(10, 12):
                wqload(i, allsl[i])
            g.dma_start(bw_s[0][:, :], bw_d[0, :, :]).then_inc(sbw[0], 16)

        def phA_tensor(_e):
            _ = None
            t = _e
            t.wait_ge(gsm, 48)
            for si, s in enumerate(KV_SLICES):
                t.wait_ge(swq[si % 2], 16 * (si // 2 + 1))
                F0 = s * SLF
                for ti in A_pos[si]:
                    if si == 0:
                        t.wait_ge(sxf[0], 16)
                    if si >= 1:
                        t.wait_ge(sce, A_sce(si - 1, ti))
                    ps = pm[:, ti * 512: ti * 512 + SLF]
                    for c in range(NCH):
                        if si == 0 and c % 8 == 0 and c > 0:
                            t.wait_ge(sxf[c // 8], 16)
                        t.matmul(ps,
                                 xfm[:, c * 1024 + ti * 128: c * 1024 + (ti + 1) * 128],
                                 wq_s[si % 2][:, c * SLF: (c + 1) * SLF],
                                 start=(c == 0), stop=False)
                    t.matmul(ps, ones_r[0:1, 0:128], bqkv_r[0:1, F0:F0 + SLF],
                             start=False, stop=True).then_inc(tmm, 1)
                    c_tmm.add()
            t.wait_ge(sce, A_KV_END)
            for qi, s in enumerate(Q_SLICES):
                si = 8 + qi
                t.wait_ge(swq[si % 2], 16 * (si // 2 + 1))
                for dsub in range(2):
                    d = qi * 2 + dsub
                    if d >= 4:
                        t.wait_ge(sce, A_q(d - 4))
                    for tq in range(2):
                        ps = pm[:, (d % 4) * 1024 + tq * 512: (d % 4) * 1024 + (tq + 1) * 512]
                        for c in range(NCH):
                            t.matmul(ps,
                                     wq_s[si % 2][:, c * SLF + dsub * 128:
                                                  c * SLF + dsub * 128 + 128],
                                     xfm[:, c * 1024 + tq * 512: c * 1024 + (tq + 1) * 512],
                                     start=(c == 0), stop=False)
                        ins = t.matmul(ps,
                                       bqkv_r[0:1, s * SLF + dsub * 128:
                                              s * SLF + dsub * 128 + 128],
                                       ones_r[0:1, tq * 512:(tq + 1) * 512],
                                       start=False, stop=True)
                        if tq == 1:
                            ins.then_inc(tmm, 1)
                            c_tmm.add()

        def phA_scalar(_e):
            _ = None
            s = _e
            for si, sl in enumerate(KV_SLICES):
                F0 = sl * SLF - 1024
                for ti in A_pos[si]:
                    s.wait_ge(tmm, A_tmm(si, ti))
                    s.activation(kvtok[:, ti * 2048 + F0: ti * 2048 + F0 + SLF],
                                 pm[:, ti * 512: ti * 512 + SLF],
                                 AF.Copy).then_inc(sce, 1)
                    c_sce.add()
            for d in range(8):
                s.wait_ge(tmm, A_q(d))
                s.activation(qfm[:, d * 1024:(d + 1) * 1024],
                             pm[:, (d % 4) * 1024:(d % 4 + 1) * 1024],
                             AF.Copy).then_inc(sce, 1)
                c_sce.add()
            s.wait_ge(ccs, CCS_K)
            s.wait_ge(tmm, A_TMM_END)
            for d in range(8):
                for rk in range(1, 4, 2):
                    s.dma_start(kfm[:, d * 4096 + rk * 1024: d * 4096 + (rk + 1) * 1024],
                                kcc_o.ap()[rk * 1024 + d * 128: rk * 1024 + (d + 1) * 128, :]
                                ).then_inc(satk[d], 16)

        def phA_sync(_e):
            _ = None
            s = _e
            for ti in range(NT):
                s.wait_ge(sce, A_sce(3, ti))
                s.dma_start(kcc_i.ap()[ti * 128:(ti + 1) * 128, :],
                            kvtok[:, ti * 2048: ti * 2048 + 1024]).then_inc(skx, 16)
            s.wait_ge(skx, 128)
            for d in range(8):
                s.dma_start(kfmo[:, d * 1024:(d + 1) * 1024],
                            kcc_i.ap()[0:1024, d * 128:(d + 1) * 128],
                            transpose=True).then_inc(skf, 16)
            s.wait_ge(skf, 128)
            for d in range(8):
                s.dma_start(kfmi.ap()[d * 128:(d + 1) * 128, :],
                            kfmo[:, d * 1024:(d + 1) * 1024]).then_inc(skfe, 16)
            for ti in range(NT):
                s.wait_ge(sce, A_sce(7, ti))
                s.dma_start(vcc_i.ap()[ti * 128:(ti + 1) * 128, :],
                            kvtok[:, ti * 2048 + 1024: ti * 2048 + 2048]).then_inc(svx, 16)
            s.wait_ge(ccs, CCS_K)
            s.wait_ge(tmm, A_TMM_END)
            for d in range(8):
                for rk in range(0, 4, 2):
                    s.dma_start(kfm[:, d * 4096 + rk * 1024: d * 4096 + (rk + 1) * 1024],
                                kcc_o.ap()[rk * 1024 + d * 128: rk * 1024 + (d + 1) * 128, :]
                                ).then_inc(satk[d], 16)

    B_tmm0 = 72

    def B_tmm_S(qt, kh):
        return B_tmm0 + 2 * qt + kh + 1

    def B_tmm_PV(half, kc):
        return B_tmm0 + 16 + half * 32 + kc + 1

    B_sce0 = 72

    def B_sce_exp(qt, kh):
        return B_sce0 + 2 * qt + kh + 1

    B_vec0 = 0

    def B_vec_dinv(qt):
        return B_vec0 + qt + 1

    def B_vec_pv(half, d):
        return B_vec0 + 8 + half * 8 + d + 1

    def PT_TGT(half, kc):
        return 16 * (half * 8 + kc // 4 + 1)

    if True:
        def phB_sync(_e):
            _ = None
            s = _e
            for qt in range(4):
                for kh in range(2):
                    s.wait_ge(sce, B_sce_exp(qt, kh))
                    s.dma_start(pd.ap()[qt * 128:(qt + 1) * 128,
                                        kh * 2048:(kh + 1) * 2048],
                                P[:, kh * 2048:(kh + 1) * 2048]).then_inc(spd[kh], 16)
            s.wait_ge(spd[0], 64)
            for kc in range(4):
                s.dma_start(ptb[kc % 4][:, :],
                            pd.ap()[0:512, kc * 128:(kc + 1) * 128],
                            transpose=True).then_inc(spt[kc % 4], 16)
            s.wait_ge(ccs, CCS_V)
            s.wait_ge(svx, 128)
            s.wait_ge(skfe, 128)
            for g2 in range(0, 8, 2):
                for kc in range(4 * g2, 4 * g2 + 4):
                    s.dma_start(vsb[:, kc * 1024:(kc + 1) * 1024],
                                vcc_o.ap()[kc * 128:(kc + 1) * 128, :]).then_inc(svg[g2], 16)
            for qt in range(4, NT):
                for kh in range(2):
                    s.wait_ge(sce, B_sce_exp(qt, kh))
                    s.dma_start(pd.ap()[qt * 128:(qt + 1) * 128,
                                        kh * 2048:(kh + 1) * 2048],
                                P[:, kh * 2048:(kh + 1) * 2048]).then_inc(spd[kh], 16)
            for half in range(2):
                for kc in range(4 if half == 0 else 0, 32):
                    idx = half * 32 + kc
                    if idx >= 4:
                        pi = idx - 4
                        s.wait_ge(tmm, B_tmm_PV(pi // 32, pi % 32))
                    s.wait_ge(spd[kc // 16], 64 if half == 0 else 128)
                    s.dma_start(ptb[kc % 4][:, :],
                                pd.ap()[half * 512:(half + 1) * 512,
                                        kc * 128:(kc + 1) * 128],
                                transpose=True).then_inc(spt[kc % 4], 16)

        def phB_scalar(_e):
            _ = None
            s = _e
            for qt in range(NT):
                s.wait_ge(tmm, B_tmm_S(qt, 0))
                if qt >= 1:
                    s.wait_ge(spd[0], 16 * qt)
                    s.wait_ge(vec, B_vec_dinv(qt - 1))
                s.activation(P[:, 0:2048], pm[:, 0:2048], AF.Exp,
                             bias=cc(C_NB8), scale=cc(C_I32),
                             accum_out=sc_(2)).then_inc(sce, 1)
                c_sce.add()
                s.wait_ge(tmm, B_tmm_S(qt, 1))
                if qt >= 1:
                    s.wait_ge(spd[1], 16 * qt)
                s.activation(P[:, 2048:4096], pm[:, 2048:4096], AF.Exp,
                             bias=cc(C_NB8), scale=cc(C_I32),
                             accum_out=sc_(5)).then_inc(sce, 1)
                c_sce.add()
            s.wait_ge(ccs, CCS_V)
            s.wait_ge(skx, 128)
            s.wait_ge(svx, 128)
            s.wait_ge(skfe, 128)
            for g2 in range(1, 8, 2):
                for kc in range(4 * g2, 4 * g2 + 4):
                    s.dma_start(vsb[:, kc * 1024:(kc + 1) * 1024],
                                vcc_o.ap()[kc * 128:(kc + 1) * 128, :]).then_inc(svg[g2], 16)

        def phB_gpsimd(_e):
            _ = None
            g = _e
            g.wait_ge(tmm, B_tmm_S(7, 1))
            _wo_gate = [None, (0, 8), (0, 24), (1, 8)]
            for i in range(4):
                if i >= 1:
                    g.wait_ge(tmm, B_tmm_PV(*_wo_gate[i]))
                g.dma_start(wo_s[:, i * 8192:(i + 1) * 8192],
                            wo_d[:, i * 8192:(i + 1) * 8192]).then_inc(swo, 16)
            g.wait_ge(tmm, B_tmm_PV(1, 31))
            g.dma_start(b2_s[:, :], b2_d[:, :]).then_inc(sab, 16)
            g.dma_start(rwt_s[:, :], rwt_d[:, :]).then_inc(sab, 16)
            if debug:
                g.dma_start(kdump.ap()[:, :], kcc_o.ap()[:, :]).then_inc(sdbg, 16)
                g.dma_start(vdump.ap()[:, :], vcc_o.ap()[:, :]).then_inc(sdbg, 16)
                g.dma_start(qdump.ap()[:, :], qfm[:, :]).then_inc(sdbg, 16)
                g.wait_ge(vec, B_vec_pv(1, 7))
                g.dma_start(aod.ap()[:, :], ao[:, :]).then_inc(sdbg, 16)
                g.wait_ge(sdbg, 64)

        def phB_tensor(_e):
            _ = None
            t = _e
            t.wait_ge(sce, 72)
            for qt in range(NT):
                for kh in range(2):
                    if qt >= 1:
                        t.wait_ge(sce, B_sce_exp(qt - 1, kh))
                    for d in range(8):
                        if qt == 0 and kh == 0:
                            t.wait_ge(satk[d], 64)
                        lhs = qfm[:, d * 1024 + qt * 128: d * 1024 + (qt + 1) * 128]
                        for kb in range(4 * kh, 4 * kh + 4):
                            ins = t.matmul(
                                pm[:, kb * 512:(kb + 1) * 512], lhs,
                                kfm[:, d * 4096 + kb * 512: d * 4096 + (kb + 1) * 512],
                                start=(d == 0), stop=(d == 7))
                            if d == 7 and kb == 4 * kh + 3:
                                ins.then_inc(tmm, 1)
                                c_tmm.add()
            t.wait_ge(sce, B_sce_exp(7, 1))
            for half in range(2):
                if half == 1:
                    for d in range(8):
                        t.wait_ge(vec, B_vec_pv(0, d))
                for kc in range(32):
                    if half == 0 and kc % 4 == 0:
                        t.wait_ge(svg[kc // 4], 64)
                    t.wait_ge(spt[kc % 4], PT_TGT(half, kc))
                    for d in range(8):
                        ins = t.matmul(
                            pm[:, d * 512:(d + 1) * 512],
                            vsb[:, kc * 1024 + d * 128: kc * 1024 + (d + 1) * 128],
                            ptb[kc % 4][:, :],
                            start=(kc == 0), stop=(kc == 31))
                        if d == 7:
                            ins.then_inc(tmm, 1)
                            c_tmm.add()

        def phB_vector(_e):
            _ = None
            v = _e
            for qt in range(NT):
                v.wait_ge(sce, B_sce_exp(qt, 1))
                v.tensor_add(dsum_tab[:, qt:qt + 1], sc_(2), sc_(5)).then_inc(vss, 1)
                v.wait_ge(vss, c_vss.add())
                v.reciprocal(dinv_tab[:, qt:qt + 1],
                             dsum_tab[:, qt:qt + 1]).then_inc(vec, 1)
                c_vec.add()
            for half in range(2):
                for d in range(8):
                    v.wait_ge(tmm, B_tmm_PV(half, 31))
                    v.tensor_copy(ao[:, d * 1024 + half * 512: d * 1024 + half * 512 + 512],
                                  pm[:, d * 512:(d + 1) * 512]).then_inc(vec, 1)
                    c_vec.add()

    C_tmm0 = 152

    def C_tmm(qt, half):
        return C_tmm0 + 1 + 2 * qt + half + 1

    C_vec0 = 24

    def C_vec_evac(qt, half):
        return C_vec0 + 1 + 5 * qt + half + 1

    def C_vec_var(qt):
        return C_vec0 + 1 + 5 * qt + 4

    def C_vec_nmr(qt):
        return C_vec0 + 1 + 5 * qt + 5

    C_sce0 = 88

    def C_sce_sq(qt):
        return C_sce0 + 2 * qt + 1

    def C_sce_apply(qt):
        return C_sce0 + 2 * qt + 2

    C_scs0 = 0

    def C_scs_rstd(qt):
        return C_scs0 + 2 * qt + 2

    if True:
        def phC_sync(_e):
            _ = None
            s = _e
            s.wait_ge(tmm, 152)
            for qt in range(2):
                s.dma_start(xtok_t[qt % 2][:, :],
                            xtok_d[qt * 128:(qt + 1) * 128, :]).then_inc(sxt[qt % 2], 16)
            for qt in range(NT):
                if qt + 2 < NT:
                    s.wait_ge(vec, C_vec_evac(qt, 1))
                    s.dma_start(xtok_t[qt % 2][:, :],
                                xtok_d[(qt + 2) * 128:(qt + 3) * 128, :]).then_inc(sxt[qt % 2], 16)
                s.wait_ge(sce, C_sce_apply(qt))
                if qt >= 1:
                    s.wait_ge(ssx, 16 * qt)
                s.dma_start(sd.ap()[qt * 128:(qt + 1) * 128, :],
                            s_t[qt % 2][:, :]).then_inc(ssx, 16)

        def phC_tensor(_e):
            _ = None
            t = _e
            t.wait_ge(swo, 64)
            t.wait_ge(vec, B_vec0 + 24)
            for j in range(NT):
                ins = t.transpose(pm[0:1, j * 128:(j + 1) * 128],
                                  dsum_tab[:, j:j + 1], ident[:, :])
                if j == NT - 1:
                    ins.then_inc(tmm, 1)
                    c_tmm.add()
            t.wait_ge(vec, C_vec0 + 1)
            for qt in range(NT):
                for half in range(2):
                    if qt >= 1:
                        t.wait_ge(vec, C_vec_evac(qt - 1, half))
                    for d in range(8):
                        lhs = ao[:, d * 1024 + qt * 128: d * 1024 + (qt + 1) * 128]
                        for cb in range(4):
                            col = half * 4 + cb
                            t.matmul(pm[:, col * 512:(col + 1) * 512], lhs,
                                     wo_s[:, d * 4096 + half * 2048 + cb * 512:
                                          d * 4096 + half * 2048 + (cb + 1) * 512],
                                     start=(d == 0), stop=False)
                    for cb in range(4):
                        col = half * 4 + cb
                        ins = t.matmul(pm[:, col * 512:(col + 1) * 512],
                                       dsumr[0:1, qt * 128:(qt + 1) * 128],
                                       bo_r[0:1, half * 2048 + cb * 512:
                                            half * 2048 + (cb + 1) * 512],
                                       start=False, stop=True)
                        if cb == 3:
                            ins.then_inc(tmm, 1)
                            c_tmm.add()

        def phC_vector(_e):
            _ = None
            v = _e
            v.wait_ge(tmm, C_tmm0 + 1)
            v.tensor_copy(dsumr[0:1, :], pm[0:1, 0:1024]).then_inc(vec, 1)
            c_vec.add()
            for qt in range(NT):
                for half in range(2):
                    v.wait_ge(tmm, C_tmm(qt, half))
                    v.wait_ge(sxt[qt % 2], 16 * (qt // 2 + 1))
                    if half == 0 and qt >= 2:
                        v.wait_ge(sce, C_sce_apply(qt - 2))
                    v.scalar_tensor_tensor(
                        y1_t[qt % 2][:, half * 2048:(half + 1) * 2048],
                        pm[:, half * 2048:(half + 1) * 2048],
                        dinv_tab[:, qt:qt + 1],
                        xtok_t[qt % 2][:, half * 2048:(half + 1) * 2048],
                        ALU.mult, ALU.add,
                        accum_out=stc(qt, half)).then_inc(vec, 1)
                    c_vec.add()
                v.wait_ge(vec, C_vec_evac(qt, 1))
                v.tensor_add(stc(qt, 2), stc(qt, 0), stc(qt, 1)).then_inc(vss, 1)
                v.wait_ge(vss, c_vss.add())
                v.tensor_scalar_mul(stc(qt, 2), stc(qt, 2), cc(C_NI4096)).then_inc(vec, 1)
                c_vec.add()
                v.wait_ge(vec, c_vec.v)
                v.wait_ge(sce, C_sce_sq(qt))
                v.tensor_scalar_mul(stc(qt, 4), stc(qt, 3), cc(C_I4096)).then_inc(vss, 1)
                v.wait_ge(vss, c_vss.add())
                v.tensor_mul(stc(qt, 5), stc(qt, 2), stc(qt, 2)).then_inc(vss, 1)
                v.wait_ge(vss, c_vss.add())
                v.tensor_sub(stc(qt, 4), stc(qt, 4), stc(qt, 5)).then_inc(vec, 1)
                c_vec.add()
                v.wait_ge(vec, c_vec.v)
                v.wait_ge(scs, C_scs_rstd(qt))
                v.tensor_mul(stc(qt, 5), stc(qt, 2), stc(qt, 4)).then_inc(vec, 1)
                c_vec.add()

        def phC_scalar(_e):
            _ = None
            s = _e
            if debug:
                s.wait_ge(sdbg, 64)
            for qt in range(NT):
                s.wait_ge(vec, C_vec_evac(qt, 1))
                s.activation(sqd[:, :], y1_t[qt % 2][:, :], AF.Square,
                             accum_out=stc(qt, 3)).then_inc(sce, 1)
                c_sce.add()
                s.wait_ge(vec, C_vec_var(qt))
                s.activation(stc(qt, 4), stc(qt, 4), AF.Ln, bias=cc(C_EPS)).then_inc(scs, 1)
                s.wait_ge(scs, c_scs.add())
                s.activation(stc(qt, 4), stc(qt, 4), AF.Exp,
                             scale=cc(C_NHALF)).then_inc(scs, 1)
                c_scs.add()
                s.wait_ge(vec, C_vec_nmr(qt))
                s.wait_ge(scs, C_scs_rstd(qt))
                if qt >= 2:
                    s.wait_ge(ssx, 16 * (qt - 1))
                s.activation(s_t[qt % 2][:, :], y1_t[qt % 2][:, :], AF.Identity,
                             bias=stc(qt, 5), scale=stc(qt, 4)).then_inc(sce, 1)
                c_sce.add()

    D_tmm0 = 169
    D_TMM_ROUTER = D_tmm0 + 1
    D_TMM_MID = D_tmm0 + 2

    def D_TMM_LT(i):
        return D_tmm0 + 2 + i + 1

    def D_TMM_FFN(fch):
        return D_tmm0 + 10 + fch + 1

    D_vec0 = 65
    D_VEC_REVAC = D_vec0 + 1

    def D_VEC_LTOK(i):
        return D_vec0 + 1 + i + 1

    D_VEC_MIDW = D_vec0 + 10

    def D_VEC_FFN(fch):
        return D_vec0 + 10 + fch + 1

    D_sce0 = 104
    D_SCE_TKEXP = D_sce0 + 1

    def SFM_TGT(c):
        return 16 * (c // 4 + 1)

    tkm = {}

    if True:
        def phD_sync(_e):
            _ = None
            s = _e
            s.wait_ge(ssx, 128)
            for c in range(NCH):
                if c >= 4:
                    s.wait_ge(ssf[c % 4], SFM_TGT(c) - 16)
                s.dma_start(sfm[:, c * 1024:(c + 1) * 1024],
                            sd.ap()[0:1024, c * 128:(c + 1) * 128],
                            transpose=True).then_inc(ssf[c % 4], 16)
            s.wait_ge(swt8, 128)
            s.dma_start(wfm[:, :], wtd.ap()[0:1024, 0:128],
                        transpose=True).then_inc(swt, 16)
            for fch in range(32):
                s.wait_ge(vec, D_VEC_FFN(fch))
                if fch >= 1:
                    s.wait_ge(sy2, 16 * fch)
                s.dma_start(y2d.ap()[fch * 128:(fch + 1) * 128, :],
                            y2_t[fch % 2][:, :]).then_inc(sy2, 16)

        def phD_tensor(_e):
            _ = None
            t = _e
            t.wait_ge(sab, 32)
            for tq in range(2):
                for c in range(NCH):
                    t.wait_ge(ssf[c % 4], SFM_TGT(c))
                    ins = t.matmul(pm[0:8, 3072 + tq * 512: 3072 + (tq + 1) * 512],
                                   rwt_s[:, c * 8:(c + 1) * 8],
                                   sfm[:, c * 1024 + tq * 512: c * 1024 + tq * 512 + 512],
                                   start=(c == 0), stop=(c == NCH - 1))
                    if tq == 1 and c == NCH - 1:
                        ins.then_inc(tmm, 1)
                        c_tmm.add()
            t.wait_ge(sa2, 16)
            for tq in range(2):
                for c in range(NCH):
                    ins = t.matmul(pm[:, 2048 + tq * 512: 2048 + (tq + 1) * 512],
                                   a2_s[:, c * 128:(c + 1) * 128],
                                   sfm[:, c * 1024 + tq * 512: c * 1024 + tq * 512 + 512],
                                   start=(c == 0), stop=(c == NCH - 1))
                    if tq == 1 and c == NCH - 1:
                        ins.then_inc(tmm, 1)
                        c_tmm.add()
            t.wait_ge(vec, D_VEC_REVAC)
            for i in range(8):
                if i >= 1:
                    t.wait_ge(vec, D_VEC_LTOK(i - 1))
                t.transpose(pm[:, 3072 + i * 8: 3072 + (i + 1) * 8],
                            logit_s[0:8, i * 128:(i + 1) * 128],
                            ident[0:8, 0:8]).then_inc(tmm, 1)
                c_tmm.add()
            for fch in range(32):
                t.wait_ge(sbw[fch % 2], 16 * (fch // 2 + 1))
                if fch >= 2:
                    t.wait_ge(vec, D_VEC_FFN(fch - 2))
                for tq in range(2):
                    for c in range(NCH):
                        t.matmul(pm[:, (fch % 2) * 1024 + tq * 512:
                                     (fch % 2) * 1024 + (tq + 1) * 512],
                                 bw_s[fch % 2][:, c * 128:(c + 1) * 128],
                                 sfm[:, c * 1024 + tq * 512: c * 1024 + tq * 512 + 512],
                                 start=(c == 0), stop=False)
                if fch == 0:
                    t.wait_ge(swt, 16)
                    t.wait_ge(vec, D_VEC_MIDW)
                for tq in range(2):
                    ins = t.matmul(pm[:, (fch % 2) * 1024 + tq * 512:
                                       (fch % 2) * 1024 + (tq + 1) * 512],
                                   b2_s[:, fch * 128:(fch + 1) * 128],
                                   midw[:, tq * 512:(tq + 1) * 512],
                                   start=False, stop=True)
                    if tq == 1:
                        ins.then_inc(tmm, 1)
                        c_tmm.add()

        def phD_vector(_e):
            _ = None
            v = _e
            SH = tk2[:, 0:64]
            EQ = tk2[:, 64:128]
            W = tk2[:, 128:192]
            E = tk2[:, 192:256]
            v.wait_ge(tmm, D_TMM_ROUTER)
            v.tensor_copy(logit_s[0:8, :], pm[0:8, 3072:4096]).then_inc(vec, 1)
            c_vec.add()
            for i in range(8):
                v.wait_ge(tmm, D_TMM_LT(i))
                v.tensor_copy(ltok[:, i * 8:(i + 1) * 8],
                              pm[:, 3072 + i * 8: 3072 + (i + 1) * 8]).then_inc(vec, 1)
                c_vec.add()
            v.wait_ge(vec, c_vec.v)
            for i in range(8):
                v.reduce_max(tk[:, i:i + 1], ltok[:, i * 8:(i + 1) * 8],
                             axis=AX.X, negate=True).then_inc(vss, 1)
                c_vss.add()
            v.wait_ge(vss, c_vss.v)
            for i in range(8):
                v.tensor_scalar_add(SH[:, i * 8:(i + 1) * 8],
                                    ltok[:, i * 8:(i + 1) * 8],
                                    tk[:, i:i + 1]).then_inc(vss, 1)
                c_vss.add()
            v.wait_ge(vss, c_vss.v)
            v.tensor_scalar(EQ, SH, cc(C_ZERO), None, ALU.is_equal).then_inc(vss, 1)
            v.wait_ge(vss, c_vss.add())
            v.scalar_tensor_tensor(W, EQ, cc(C_NBIG), SH, ALU.mult, ALU.add).then_inc(vss, 1)
            v.wait_ge(vss, c_vss.add())
            for i in range(8):
                v.reduce_max(tk[:, 8 + i:9 + i], W[:, i * 8:(i + 1) * 8],
                             axis=AX.X).then_inc(vss, 1)
                c_vss.add()
            v.wait_ge(vss, c_vss.v)
            for i in range(8):
                v.tensor_scalar(EQ[:, i * 8:(i + 1) * 8], SH[:, i * 8:(i + 1) * 8],
                                tk[:, 8 + i:9 + i], None, ALU.is_ge).then_inc(vss, 1)
                c_vss.add()
            tkm["exp_in"] = c_vss.v
            v.wait_ge(sce, D_SCE_TKEXP)
            v.tensor_mul(W, E, EQ).then_inc(vss, 1)
            v.wait_ge(vss, c_vss.add())
            for i in range(8):
                v.reduce_sum(tk[:, 16 + i:17 + i], W[:, i * 8:(i + 1) * 8],
                             axis=AX.X).then_inc(vss, 1)
                c_vss.add()
            v.wait_ge(vss, c_vss.v)
            v.reciprocal(tk[:, 24:32], tk[:, 16:24]).then_inc(vss, 1)
            v.wait_ge(vss, c_vss.add())
            v.tensor_scalar_mul(tk[:, 24:32], tk[:, 24:32], cc(C_SCL)).then_inc(vss, 1)
            v.wait_ge(vss, c_vss.add())
            for i in range(8):
                v.tensor_scalar_mul(W[:, i * 8:(i + 1) * 8], W[:, i * 8:(i + 1) * 8],
                                    tk[:, 24 + i:25 + i]).then_inc(vss, 1)
                c_vss.add()
            v.wait_ge(vss, c_vss.v)
            for i in range(8):
                for e in range(8):
                    v.tensor_scalar_mul(wet8[:, i * 128 + e * 16: i * 128 + (e + 1) * 16],
                                        ones16[:, :],
                                        W[:, i * 8 + e: i * 8 + e + 1]).then_inc(vss, 1)
                    c_vss.add()
            tkm["wet8"] = c_vss.v
            v.wait_ge(swt, 16)
            v.wait_ge(tmm, D_TMM_MID)
            v.tensor_mul(midw[:, :], pm[:, 2048:3072], wfm[:, :]).then_inc(vec, 1)
            c_vec.add()
            assert c_vec.v == D_VEC_MIDW
            for fch in range(32):
                v.wait_ge(tmm, D_TMM_FFN(fch))
                if fch >= 2:
                    v.wait_ge(sy2, 16 * (fch - 1))
                v.scalar_tensor_tensor(
                    y2_t[fch % 2][:, :],
                    pm[:, (fch % 2) * 1024:(fch % 2 + 1) * 1024],
                    cc(C_ONE),
                    sfm[:, fch * 1024:(fch + 1) * 1024],
                    ALU.mult, ALU.add).then_inc(vec, 1)
                c_vec.add()

        def phD_gpsimd(_e):
            _ = None
            g = _e
            g.wait_ge(sce, C_sce_apply(7))
            g.wait_ge(ssx, 128)
            g.dma_start(a2_s[:, :], a2_d[:, :]).then_inc(sa2, 16)
            g.dma_start(bw_s[1][:, :], bw_d[1, :, :]).then_inc(sbw[1], 16)
            g.wait_ge(vss, tkm["wet8"])
            for i in range(8):
                g.dma_start(wtd.ap()[i * 128:(i + 1) * 128, :],
                            wet8[:, i * 128:(i + 1) * 128]).then_inc(swt8, 16)
            for fch in range(2, 32):
                g.wait_ge(vec, D_VEC_FFN(fch - 2))
                g.dma_start(bw_s[fch % 2][:, :], bw_d[fch, :, :]).then_inc(sbw[fch % 2], 16)

        def phD_scalar(_e):
            _ = None
            s = _e
            s.wait_ge(vss, tkm["exp_in"])
            s.activation(tk2[:, 192:256], tk2[:, 0:64], AF.Exp).then_inc(sce, 1)
            c_sce.add()

    E_vec0 = 107

    def E_vec_stats(qt):
        return E_vec0 + 2 * qt + 1

    def E_vec_nmr(qt):
        return E_vec0 + 2 * qt + 2

    E_sce0 = 105

    def E_sce_apply(qt):
        return E_sce0 + qt + 1

    E_scs0 = 16

    def E_scs_rstd(qt):
        return E_scs0 + 2 * qt + 2

    if True:
        def phE_sync(_e):
            _ = None
            s = _e
            s.wait_ge(sy2, 512)
            for qt in range(4):
                s.dma_start(y2tok_t[qt % 4][:, :],
                            y2d.ap()[0:4096, qt * 128:(qt + 1) * 128],
                            transpose=True).then_inc(sy2t[qt % 4], 16)
            for qt in range(4, NT):
                s.wait_ge(sce, E_sce_apply(qt - 4))
                s.dma_start(y2tok_t[qt % 4][:, :],
                            y2d.ap()[0:4096, qt * 128:(qt + 1) * 128],
                            transpose=True).then_inc(sy2t[qt % 4], 16)
            s.wait_ge(sot[0], 64)
            s.wait_ge(sot[1], 64)

        def phE_vector(_e):
            _ = None
            v = _e
            for qt in range(NT):
                v.wait_ge(sy2t[qt % 4], 16 * (qt // 4 + 1))
                for j in range(8):
                    v.bn_stats(tk[:, j * 6:(j + 1) * 6],
                               y2tok_t[qt % 4][:, j * 512:(j + 1) * 512]).then_inc(vss, 1)
                    c_vss.add()
                v.wait_ge(vss, c_vss.v)
                v.bn_aggr(tk[:, 48:50], tk[:, 0:48]).then_inc(vss, 1)
                v.wait_ge(vss, c_vss.add())
                v.tensor_scalar_mul(stc(qt, 2), tk[:, 48:49], cc(C_NONE)).then_inc(vss, 1)
                v.wait_ge(vss, c_vss.add())
                v.tensor_copy(stc(qt, 4), tk[:, 49:50]).then_inc(vec, 1)
                c_vec.add()
                v.wait_ge(vec, c_vec.v)
                v.wait_ge(scs, E_scs_rstd(qt))
                v.tensor_mul(stc(qt, 5), stc(qt, 2), stc(qt, 4)).then_inc(vec, 1)
                c_vec.add()

        def phE_scalar(_e):
            _ = None
            s = _e
            for qt in range(NT):
                s.wait_ge(vec, E_vec_stats(qt))
                s.activation(stc(qt, 4), stc(qt, 4), AF.Ln, bias=cc(C_EPS)).then_inc(scs, 1)
                s.wait_ge(scs, c_scs.add())
                s.activation(stc(qt, 4), stc(qt, 4), AF.Exp,
                             scale=cc(C_NHALF)).then_inc(scs, 1)
                c_scs.add()
                s.wait_ge(vec, E_vec_nmr(qt))
                s.wait_ge(scs, E_scs_rstd(qt))
                if qt >= 2:
                    s.wait_ge(sot[qt % 2], 16 * ((qt - 2) // 2 + 1))
                s.activation(outt[qt % 2][:, :], y2tok_t[qt % 4][:, :], AF.Identity,
                             bias=stc(qt, 5), scale=stc(qt, 4)).then_inc(sce, 1)
                c_sce.add()
                s.wait_ge(sce, E_sce_apply(qt))
                s.dma_start(out_d[qt * 128:(qt + 1) * 128, :],
                            outt[qt % 2][:, :]).then_inc(sot[qt % 2], 16)


    def _seq(*fns):
        def run(e):
            for f in fns:
                if f is not None:
                    f(e)
        return run


    with nc.Block() as blk:
        blk.vector(_seq(phB_vector, phC_vector, phD_vector, phE_vector))
        blk.tensor(_seq(phA_tensor, phB_tensor, phC_tensor, phD_tensor))
        blk.scalar(_seq(phA_scalar, phB_scalar, phC_scalar, phD_scalar, phE_scalar))
        blk.sync(_seq(phA_sync, phB_sync, phC_sync, phD_sync, phE_sync))
        blk.gpsimd(_seq(phA_gpsimd, phB_gpsimd, phD_gpsimd))
    assert c_tmm.v == 211, c_tmm.v
    assert c_sce.v == 113, c_sce.v
    assert c_vec.v == 123, c_vec.v
    assert c_scs.v == 32, c_scs.v

    ctx.close()
    return nc


def _host_layouts(L0_attn_x, L0_attn_Wqkv, L0_attn_bqkv, L0_attn_Wo, L0_attn_bo,
                  L0_ffn_base_W, L0_ffn_router_W, L0_ffn_lora_A, L0_ffn_lora_B):
    x = np.asarray(L0_attn_x, dtype=np.float32)
    Wqkv = np.asarray(L0_attn_Wqkv, dtype=np.float32)
    bqkv = np.asarray(L0_attn_bqkv, dtype=np.float32)
    Wo = np.asarray(L0_attn_Wo, dtype=np.float32)
    bo = np.asarray(L0_attn_bo, dtype=np.float32)
    bW = np.asarray(L0_ffn_base_W, dtype=np.float32)
    rW = np.asarray(L0_ffn_router_W, dtype=np.float32)
    lA = np.asarray(L0_ffn_lora_A, dtype=np.float32)
    lB = np.asarray(L0_ffn_lora_B, dtype=np.float32)

    wq3 = Wqkv.reshape(32, 128, 3072).transpose(1, 0, 2)
    wq_host = np.ascontiguousarray(
        np.stack([wq3[:, :, s * SLF:(s + 1) * SLF].reshape(128, 32 * SLF)
                  for s in range(12)], axis=0))
    wo_host = np.ascontiguousarray(
        Wo.reshape(8, 128, 4096).transpose(1, 0, 2).reshape(128, 32768))
    BT = np.ascontiguousarray(bW.T)
    bw_host = np.ascontiguousarray(
        np.stack([BT.reshape(32, 128, 4096)[:, :, f * 128:(f + 1) * 128]
                  .transpose(1, 0, 2).reshape(128, 4096) for f in range(32)], axis=0))
    A2 = lA.transpose(1, 0, 2).reshape(4096, 128)
    a2_host = np.ascontiguousarray(
        A2.reshape(32, 128, 128).transpose(1, 0, 2).reshape(128, 4096))
    b2_host = np.ascontiguousarray(lB.reshape(128, 4096))
    rwt_host = np.ascontiguousarray(
        rW.T.reshape(32, 128, 8).transpose(1, 0, 2).reshape(128, 256))
    bqkv_host = np.ascontiguousarray(bqkv[None, :])
    bo_host = np.ascontiguousarray(bo[None, :])
    ident_host = np.eye(128, dtype=np.float32)

    in_maps = []
    for core in range(8):
        seq, q0 = core // 4, (core % 4) * 1024
        xt = np.ascontiguousarray(x[seq].T)
        xfm_host = np.ascontiguousarray(
            xt[:, q0:q0 + 1024].reshape(32, 128, 1024)
            .transpose(1, 0, 2).reshape(128, 32768))
        xtok_host = np.ascontiguousarray(x[seq, q0:q0 + 1024, :])
        in_maps.append({
            "xfm": xfm_host, "xtok": xtok_host, "wq": wq_host, "wo": wo_host,
            "bw": bw_host, "a2": a2_host, "b2": b2_host, "rwt": rwt_host,
            "bqkv": bqkv_host, "bo": bo_host, "ident": ident_host,
        })
    return in_maps


def kernel(**inputs):
    in_maps = _host_layouts(**inputs)
    nc = build_nc(debug=bool(int(os.environ.get("MK_DEBUG", "0"))))
    res = run_bass_kernel_spmd(nc, in_maps, core_ids=list(range(8)))
    out = np.empty((2, 4096, 4096), dtype=np.float32)
    for core in range(8):
        seq, q0 = core // 4, (core % 4) * 1024
        out[seq, q0:q0 + 1024, :] = np.asarray(res.results[core]["out"],
                                               dtype=np.float32)
    return out
